# version 01
# baseline (speedup 1.0000x reference)
import contextlib

import numpy as np
import ml_dtypes
import concourse.bass as bass
import concourse.mybir as mybir
from concourse.bass_utils import run_bass_kernel_spmd

F32 = mybir.dt.float32
BF16 = mybir.dt.bfloat16
I32 = mybir.dt.int32
AF = mybir.ActivationFunctionType
ALU = mybir.AluOpType
AX = mybir.AxisListType

BF16NP = ml_dtypes.bfloat16

B, S, DM = 4, 4096, 2048
TPC = 2048
I0 = 5504
I1 = 5632
QLAT, KVLAT = 768, 512
TWO_PI = float(2.0 * np.pi)
LOG1E4 = float(np.log(10000.0))

QTILES = [[2 * i for i in range(16)], [2 * i + 1 for i in range(16)]]


def own_rows(e):
    return np.concatenate([np.arange(qt * 128, qt * 128 + 128) for qt in QTILES[e]])


_UN = [0]


def uname(base):
    _UN[0] += 1
    return f"{base}_{_UN[0]}"


def cdiv(a, b):
    return (a + b - 1) // b



_DATAPATH_OPS = (
    "tensor_copy", "tensor_tensor", "tensor_scalar", "tensor_single_scalar",
    "tensor_add", "tensor_sub", "tensor_mul", "tensor_max", "tensor_relu",
    "tensor_reduce", "reduce_max", "reduce_sum", "scalar_tensor_tensor",
    "tensor_tensor_scan", "memset", "memzero", "iota", "affine_select",
    "activation", "copy", "mul", "add", "sign", "square", "sqrt",
    "reciprocal", "select", "copy_predicated", "bn_stats", "bn_aggr",
    "pool", "pool_avg", "pool_max", "stream_shuffle", "match_replace",
)


class AD:

    def __init__(self, e):
        self.e = e
        self.first = True

    def __getattr__(self, name):
        f = getattr(self.e, name)
        if name in _DATAPATH_OPS:
            def wrapped(*a, **k):
                self.e.drain()
                return f(*a, **k)
            return wrapped
        return f


class Ctr:
    def __init__(self, h):
        self.h = h
        self.v = 0

    def inc(self, n=1):
        self.v += n
        return self.v



def build_dequant():
    nc = bass.Bass("TRN2", target_bir_lowering=False)

    g0 = nc.dram_tensor("g0", [688, 2048], I32, kind="ExternalInput").ap()
    u0 = nc.dram_tensor("u0", [688, 2048], I32, kind="ExternalInput").ap()
    d0 = nc.dram_tensor("d0", [256, 5504], I32, kind="ExternalInput").ap()
    lut = nc.dram_tensor("lut", [16], F32, kind="ExternalInput").ap()
    g1 = nc.dram_tensor("g1", [704, 2048], I32, kind="ExternalInput").ap()
    u1 = nc.dram_tensor("u1", [704, 2048], I32, kind="ExternalInput").ap()
    d1 = nc.dram_tensor("d1", [256, 5632], I32, kind="ExternalInput").ap()
    sg = nc.dram_tensor("sg", [704], F32, kind="ExternalInput").ap()
    su = nc.dram_tensor("su", [704], F32, kind="ExternalInput").ap()
    sd = nc.dram_tensor("sd", [256], F32, kind="ExternalInput").ap()

    g0t = nc.dram_tensor("g0t", [2048, 688], BF16, kind="ExternalOutput").ap()
    u0t = nc.dram_tensor("u0t", [2048, 688], BF16, kind="ExternalOutput").ap()
    d0t = nc.dram_tensor("d0t", [5504, 256], BF16, kind="ExternalOutput").ap()
    g1t = nc.dram_tensor("g1t", [2048, 704], BF16, kind="ExternalOutput").ap()
    u1t = nc.dram_tensor("u1t", [2048, 704], BF16, kind="ExternalOutput").ap()
    d1t = nc.dram_tensor("d1t", [5632, 256], BF16, kind="ExternalOutput").ap()

    with contextlib.ExitStack() as st:
        idx_i = st.enter_context(nc.sbuf_tensor("idx_i", [128, 5632], I32))
        idx_b = st.enter_context(nc.sbuf_tensor("idx_b", [128, 5632], BF16))
        w_tmp = st.enter_context(nc.sbuf_tensor("w_tmp", [128, 5632], BF16))
        w_nat = st.enter_context(nc.sbuf_tensor("w_nat", [128, 6 * 2048], BF16))
        wd_nat = st.enter_context(nc.sbuf_tensor("wd_nat", [128, 2 * 5632], BF16))
        wt_asm = st.enter_context(nc.sbuf_tensor("wt_asm", [128, 704], BF16))
        lut_row = st.enter_context(nc.sbuf_tensor("lut_row", [1, 16], F32))
        ones1 = st.enter_context(nc.sbuf_tensor("ones1", [1, 128], F32))
        lut_b = st.enter_context(nc.sbuf_tensor("lut_b", [128, 16], F32))
        scol = st.enter_context(nc.sbuf_tensor("scol", [128, 14], F32))
        iota_i = st.enter_context(nc.sbuf_tensor("iota_i", [128, 128], I32))
        ident_b = st.enter_context(nc.sbuf_tensor("ident_b", [128, 128], BF16))
        ps_b = st.enter_context(nc.psum_tensor("ps_b", [128, 16], F32))
        ps_t = st.enter_context(nc.psum_tensor("ps_t", [128, 128], BF16))

        ld = Ctr(st.enter_context(nc.semaphore("ld")))
        ve = Ctr(st.enter_context(nc.semaphore("ve")))
        pe = Ctr(st.enter_context(nc.semaphore("pe")))
        se = Ctr(st.enter_context(nc.semaphore("se")))
        so = Ctr(st.enter_context(nc.semaphore("so")))
        block = st.enter_context(nc.Block())

        @block.sync
        def _(e):
            e.dma_start(out=lut_row[:], in_=lut[None, :]).then_inc(ld.h, 16)
            ld.inc(16)
            for col in range(6):
                n = min(128, 704 - col * 128)
                e.dma_start(out=scol[0:n, col : col + 1],
                            in_=sg[col * 128 : col * 128 + n, None]).then_inc(ld.h, 16)
                e.dma_start(out=scol[0:n, 6 + col : 7 + col],
                            in_=su[col * 128 : col * 128 + n, None]).then_inc(ld.h, 16)
                ld.inc(32)
            for col in range(2):
                e.dma_start(out=scol[:, 12 + col : 13 + col],
                            in_=sd[col * 128 : col * 128 + 128, None]).then_inc(ld.h, 16)
                ld.inc(16)
        t_setup_ld = ld.v

        @block.gpsimd
        def _(e):
            e = AD(e)
            e.iota(iota_i[:], [[1, 128]], channel_multiplier=-1).then_inc(ve.h, 1)
        v_iota = ve.inc()

        @block.vector
        def _(e):
            e = AD(e)
            e.wait_ge(ve.h, v_iota)
            e.memset(ones1[:], 1.0)
            e.tensor_scalar(ident_b[:], iota_i[:], 0.0, None, ALU.is_equal)
            e.wait_ge(ld.h, t_setup_ld)
            e.engine_nop().then_inc(ve.h, 1)
        v_setup = ve.inc()

        @block.tensor
        def _(e):
            e.wait_ge(ve.h, v_setup)
            e.matmul(ps_b[:], ones1[:], lut_row[:], start=True, stop=True).then_inc(pe.h, 1)
        p_lut = pe.inc()

        @block.vector
        def _(e):
            e = AD(e)
            e.wait_ge(pe.h, p_lut)
            e.tensor_copy(lut_b[:], ps_b[:]).then_inc(ve.h, 1)
        v_lutb = ve.inc()

        def dequant_tiles(idx_dram, nrows_total, width, wdst, scol_base):
            ntiles = cdiv(nrows_total, 128)
            for rt in range(ntiles):
                n = min(128, nrows_total - rt * 128)

                @block.sync
                def _(e, rt=rt, n=n):
                    e.wait_ge(ve.h, ve.v)
                    e.dma_start(out=idx_i[0:n, 0:width],
                                in_=idx_dram[rt * 128 : rt * 128 + n, :]).then_inc(ld.h, 16)
                t_ld = ld.inc(16)
                dst = wdst[0:n, rt * width : rt * width + width]

                @block.vector
                def _(e, rt=rt, n=n, dst=dst, t_ld=t_ld):
                    e = AD(e)
                    e.wait_ge(ld.h, t_ld)
                    e.wait_ge(pe.h, pe.v)
                    if scol_base is None:
                        e.wait_ge(ve.h, v_lutb)
                        e.tensor_copy(idx_b[0:n, 0:width], idx_i[0:n, 0:width])
                        for j in range(16):
                            tgt = dst if j == 0 else w_tmp[0:n, 0:width]
                            e.tensor_scalar(tgt, idx_b[0:n, 0:width], float(j),
                                            lut_b[0:n, j : j + 1], ALU.is_equal, ALU.mult)
                            if j > 0:
                                e.tensor_tensor(dst, dst, w_tmp[0:n, 0:width], ALU.add)
                    else:
                        e.tensor_scalar(dst, idx_i[0:n, 0:width], 128.0,
                                        scol[0:n, scol_base + rt : scol_base + rt + 1],
                                        ALU.subtract, ALU.mult)
                    e.engine_nop().then_inc(ve.h, 1)
                ve.inc()
            return ve.v

        def transpose_out(wdst, nrows_total, width, out_dram, v_ready):
            ntiles = cdiv(nrows_total, 128)
            nchunks = cdiv(width, 128)
            for hc in range(nchunks):
                w = min(128, width - hc * 128)
                for rt in range(ntiles):
                    n = min(128, nrows_total - rt * 128)

                    @block.tensor
                    def _(e, hc=hc, rt=rt, n=n, w=w):
                        if hc == 0 and rt == 0:
                            e.wait_ge(ve.h, v_ready)
                        e.wait_ge(se.h, se.v)
                        src = wdst[0:n, rt * width + hc * 128 : rt * width + hc * 128 + w]
                        e.transpose(ps_t[0:w, 0:n], src, ident_b[0:n, 0:n]).then_inc(pe.h, 1)
                    p_t = pe.inc()

                    @block.scalar
                    def _(e, hc=hc, rt=rt, n=n, w=w, p_t=p_t):
                        e = AD(e)
                        if rt == 0:
                            e.wait_ge(so.h, so.v)
                        e.wait_ge(pe.h, p_t)
                        e.copy(wt_asm[0:w, rt * 128 : rt * 128 + n], ps_t[0:w, 0:n]).then_inc(se.h, 1)
                    se.inc()

                @block.gpsimd
                def _(e, hc=hc, w=w):
                    e = AD(e)
                    e.wait_ge(se.h, se.v)
                    e.dma_start(out=out_dram[hc * 128 : hc * 128 + w, 0:nrows_total],
                                in_=wt_asm[0:w, 0:nrows_total]).then_inc(so.h, 16)
                so.inc(16)

        v = dequant_tiles(g1, 704, 2048, w_nat, 0)
        transpose_out(w_nat, 704, 2048, g1t, v)
        v = dequant_tiles(u1, 704, 2048, w_nat, 6)
        transpose_out(w_nat, 704, 2048, u1t, v)
        v = dequant_tiles(d1, 256, 5632, wd_nat, 12)
        transpose_out(wd_nat, 256, 5632, d1t, v)
        v = dequant_tiles(g0, 688, 2048, w_nat, None)
        transpose_out(w_nat, 688, 2048, g0t, v)
        v = dequant_tiles(u0, 688, 2048, w_nat, None)
        transpose_out(w_nat, 688, 2048, u0t, v)
        v = dequant_tiles(d0, 256, 5504, wd_nat, None)
        transpose_out(wd_nat, 256, 5504, d0t, v)

        @block.gpsimd
        def _(e):
            e = AD(e)
            e.wait_ge(so.h, so.v)

    return nc





def build_a0(debug=False):
    nc = bass.Bass("TRN2", target_bir_lowering=False)
    xb = nc.dram_tensor("xb", [S, DM], F32, kind="ExternalInput").ap()
    x_own = nc.dram_tensor("x_own", [TPC, DM], F32, kind="ExternalInput").ap()
    pos_own = nc.dram_tensor("pos_own", [TPC], I32, kind="ExternalInput").ap()
    thr = nc.dram_tensor("thr", [16], F32, kind="ExternalInput").ap()
    wq = nc.dram_tensor("wq", [DM, DM], F32, kind="ExternalInput").ap()
    wk = nc.dram_tensor("wk", [DM, 512], F32, kind="ExternalInput").ap()
    wv = nc.dram_tensor("wv", [DM, 512], F32, kind="ExternalInput").ap()
    hs = nc.dram_tensor("hs", [16], F32, kind="ExternalInput").ap()
    o_ts = nc.dram_tensor("o_ts", [TPC, DM], BF16, kind="ExternalOutput").ap()
    if debug:
        dbg_kf = nc.dram_tensor("dbg_kf", [128, 4 * S], BF16, kind="ExternalOutput").ap()
        dbg_vt = nc.dram_tensor("dbg_vt", [128, 32 * 516], BF16, kind="ExternalOutput").ap()
        dbg_cos = nc.dram_tensor("dbg_cos", [128, S], BF16, kind="ExternalOutput").ap()
        dbg_sin = nc.dram_tensor("dbg_sin", [128, S], BF16, kind="ExternalOutput").ap()
        dbg_cq = nc.dram_tensor("dbg_cq", [128, TPC], BF16, kind="ExternalOutput").ap()
        dbg_sq = nc.dram_tensor("dbg_sq", [128, TPC], BF16, kind="ExternalOutput").ap()
        dbg_qf = nc.dram_tensor("dbg_qf", [128, 8 * TPC], BF16, kind="ExternalOutput").ap()

    with contextlib.ExitStack() as st:
        ident_b = st.enter_context(nc.sbuf_tensor("ident_b", [128, 128], BF16))
        iota_i = st.enter_context(nc.sbuf_tensor("iota_i", [128, 512], I32))
        cos_t = st.enter_context(nc.sbuf_tensor("cos_t", [128, S], BF16))
        sin_t = st.enter_context(nc.sbuf_tensor("sin_t", [128, S], BF16))
        cos_q = st.enter_context(nc.sbuf_tensor("cos_q", [128, TPC], BF16))
        sin_q = st.enter_context(nc.sbuf_tensor("sin_q", [128, TPC], BF16))
        rot_m = st.enter_context(nc.sbuf_tensor("rot_m", [128, 128], BF16))
        hs_b = st.enter_context(nc.sbuf_tensor("hs_b", [128, 16], F32))
        thr_b = st.enter_context(nc.sbuf_tensor("thr_b", [128, 16], F32))
        mask_c = st.enter_context(nc.sbuf_tensor("mask_c", [128, 512], BF16))
        ones1 = st.enter_context(nc.sbuf_tensor("ones1", [1, 128], F32))
        kf = st.enter_context(nc.sbuf_tensor("kf", [128, 4 * S], BF16))
        vt = st.enter_context(nc.sbuf_tensor("vt", [128, 32 * 516], BF16))

        ld = Ctr(st.enter_context(nc.semaphore("ld")))
        gc = Ctr(st.enter_context(nc.semaphore("gc")))
        gi = Ctr(st.enter_context(nc.semaphore("gi")))
        pe = Ctr(st.enter_context(nc.semaphore("pe")))
        ve = Ctr(st.enter_context(nc.semaphore("ve")))
        se = Ctr(st.enter_context(nc.semaphore("se")))
        so = Ctr(st.enter_context(nc.semaphore("so")))
        block = st.enter_context(nc.Block())

        def barrier():
            caps = [(c, c.v) for c in (ld, gc, gi, pe, ve, se, so)]
            for engname in ("sync", "gpsimd", "vector", "scalar", "tensor"):
                @getattr(block, engname)
                def _(e):
                    for c, v in caps:
                        e.wait_ge(c.h, v)

        with contextlib.ExitStack() as st2:
            tbA = st2.enter_context(nc.sbuf_tensor("tbA", [128, S], F32))
            tbB = st2.enter_context(nc.sbuf_tensor("tbB", [128, S], F32))
            tbI = st2.enter_context(nc.sbuf_tensor("tbI", [128, S], I32))
            tbC = st2.enter_context(nc.sbuf_tensor("tbC", [128, S], F32))
            invf = st2.enter_context(nc.sbuf_tensor("invf", [128, 1], F32))
            pidx = st2.enter_context(nc.sbuf_tensor("pidx", [128, 1], I32))
            hs_row = st2.enter_context(nc.sbuf_tensor("hs_row", [1, 16], F32))
            thr_row = st2.enter_context(nc.sbuf_tensor("thr_row", [1, 16], F32))
            pos_row = st2.enter_context(nc.sbuf_tensor("pos_row", [1, TPC], I32))
            pos_f = st2.enter_context(nc.sbuf_tensor("pos_f", [1, TPC], F32))
            ps_h = st2.enter_context(nc.psum_tensor("ps_h", [128, 16], F32))
            ps_pb = st2.enter_context(nc.psum_tensor("ps_pb", [128, 512], F32))

            @block.sync
            def _(e):
                e.dma_start(out=hs_row[:], in_=hs[None, :]).then_inc(ld.h, 16)
                ld.inc(16)
                e.dma_start(out=thr_row[:], in_=thr[None, :]).then_inc(ld.h, 16)
                ld.inc(16)
                e.dma_start(out=pos_row[:], in_=pos_own[None, :]).then_inc(ld.h, 16)
                ld.inc(16)
            t_hs = ld.v

            @block.gpsimd
            def _(e):
                e = AD(e)
                e.iota(iota_i[:], [[1, 512]], channel_multiplier=-1)
                e.iota(pidx[:], [[0, 1]], channel_multiplier=1)
                e.iota(tbI[:], [[1, S]], channel_multiplier=0)
                e.engine_nop().then_inc(gi.h, 1)
            g_iota = gi.inc()

            @block.vector
            def _(e):
                e = AD(e)
                e.wait_ge(gi.h, g_iota)
                e.memset(ones1[:], 1.0)
                e.tensor_scalar(ident_b[:], iota_i[:, 0:128], 0.0, None, ALU.is_equal)
                e.tensor_scalar(rot_m[:], iota_i[:, 0:128], 64.0, None, ALU.is_equal)
                e.tensor_scalar(tbC[:, 0:128], iota_i[:, 0:128], -64.0, -1.0,
                                ALU.is_equal, ALU.mult)
                e.tensor_tensor(rot_m[:], rot_m[:], tbC[0:128, 0:128], ALU.add)
                e.tensor_copy(invf[:], pidx[:])
                e.tensor_scalar(tbC[:, 0:1], invf[:], 63.5, 64.0, ALU.is_gt, ALU.mult)
                e.tensor_tensor(invf[:], invf[:], tbC[:, 0:1], ALU.subtract)
                e.engine_nop().then_inc(ve.h, 1)
            v_pre = ve.inc()

            @block.scalar
            def _(e):
                e = AD(e)
                e.wait_ge(se.h, se.v)
                e.wait_ge(ve.h, v_pre)
                e.activation(invf[:], invf[:], AF.Exp, scale=-LOG1E4 / 64.0).then_inc(se.h, 1)
            s_invf = se.inc()

            def emit_phase(e, m_ap, dst, w):
                e.tensor_copy(tbI[:, 0:w], m_ap)
                e.tensor_copy(dst, tbI[:, 0:w])
                e.tensor_tensor(dst, m_ap, dst, ALU.subtract)
                e.tensor_scalar(tbC[:, 0:w], dst, 0.5, None, ALU.is_ge)
                e.tensor_tensor(dst, dst, tbC[:, 0:w], ALU.subtract)

            @block.vector
            def _(e):
                e = AD(e)
                e.wait_ge(se.h, s_invf)
                e.tensor_copy(tbA[:], tbI[:])
                e.tensor_scalar(tbA[:], tbA[:], invf[:], None, ALU.mult)
                e.tensor_scalar(tbB[:], tbA[:], 1.0 / TWO_PI, None, ALU.mult)
                emit_phase(e, tbB[:], tbA[:], S)
                e.engine_nop().then_inc(ve.h, 1)
            v_ph = ve.inc()

            @block.scalar
            def _(e):
                e = AD(e)
                e.wait_ge(ve.h, v_ph)
                e.activation(sin_t[:], tbA[:], AF.Sin, scale=TWO_PI).then_inc(se.h, 1)
            se.inc()

            @block.vector
            def _(e):
                e = AD(e)
                e.wait_ge(se.h, se.v)
                e.tensor_scalar(tbB[:], tbB[:], 0.25, None, ALU.add)
                emit_phase(e, tbB[:], tbA[:], S)
                e.engine_nop().then_inc(ve.h, 1)
            v_phc = ve.inc()

            @block.scalar
            def _(e):
                e = AD(e)
                e.wait_ge(ve.h, v_phc)
                e.activation(cos_t[:], tbA[:], AF.Sin, scale=TWO_PI).then_inc(se.h, 1)
            se.inc()

            @block.tensor
            def _(e):
                e.wait_ge(ld.h, t_hs)
                e.wait_ge(ve.h, v_pre)
                e.matmul(ps_h[:], ones1[:], hs_row[:], start=True, stop=True).then_inc(pe.h, 1)
            p_hs = pe.inc()

            @block.vector
            def _(e):
                e = AD(e)
                e.wait_ge(pe.h, p_hs)
                e.tensor_copy(hs_b[:], ps_h[:])
                e.tensor_copy(pos_f[:], pos_row[:])
                e.engine_nop().then_inc(ve.h, 1)
            v_posf = ve.inc()

            @block.tensor
            def _(e):
                e.wait_ge(ve.h, v_posf)
                e.matmul(ps_h[:], ones1[:], thr_row[:], start=True, stop=True).then_inc(pe.h, 1)
            p_thr = pe.inc()

            @block.vector
            def _(e):
                e = AD(e)
                e.wait_ge(pe.h, p_thr)
                e.tensor_copy(thr_b[:], ps_h[:]).then_inc(ve.h, 1)
            ve.inc()

            for c4 in range(4):
                @block.tensor
                def _(e, c4=c4):
                    e.wait_ge(ve.h, ve.v)
                    e.matmul(ps_pb[:], ones1[:],
                             pos_f[:, c4 * 512 : (c4 + 1) * 512],
                             start=True, stop=True).then_inc(pe.h, 1)
                p_pb = pe.inc()

                @block.vector
                def _(e, c4=c4, p_pb=p_pb):
                    e = AD(e)
                    e.wait_ge(pe.h, p_pb)
                    e.tensor_copy(tbB[:, c4 * 512 : (c4 + 1) * 512], ps_pb[:]).then_inc(ve.h, 1)
                ve.inc()
            v_posb = ve.v

            @block.vector
            def _(e):
                e = AD(e)
                e.wait_ge(ve.h, v_posb)
                e.wait_ge(se.h, se.v)
                e.tensor_scalar(tbA[:, 0:TPC], tbB[:, 0:TPC], invf[:], None, ALU.mult)
                e.tensor_scalar(tbB[:, 0:TPC], tbA[:, 0:TPC], 1.0 / TWO_PI, None, ALU.mult)
                emit_phase(e, tbB[:, 0:TPC], tbA[:, 0:TPC], TPC)
                e.engine_nop().then_inc(ve.h, 1)
            v_qph = ve.inc()

            @block.scalar
            def _(e):
                e = AD(e)
                e.wait_ge(ve.h, v_qph)
                e.activation(sin_q[:], tbA[:, 0:TPC], AF.Sin, scale=TWO_PI).then_inc(se.h, 1)
            se.inc()

            @block.vector
            def _(e):
                e = AD(e)
                e.wait_ge(se.h, se.v)
                e.tensor_scalar(tbB[:, 0:TPC], tbB[:, 0:TPC], 0.25, None, ALU.add)
                emit_phase(e, tbB[:, 0:TPC], tbA[:, 0:TPC], TPC)
                e.engine_nop().then_inc(ve.h, 1)
            v_qphc = ve.inc()

            @block.scalar
            def _(e):
                e = AD(e)
                e.wait_ge(ve.h, v_qphc)
                e.activation(cos_q[:], tbA[:, 0:TPC], AF.Sin, scale=TWO_PI).then_inc(se.h, 1)
            se.inc()

        barrier()

        with contextlib.ExitStack() as st2:
            wkb = st2.enter_context(nc.sbuf_tensor("wkb", [128, 16 * 512], BF16))
            wvb = st2.enter_context(nc.sbuf_tensor("wvb", [128, 16 * 512], BF16))
            x_t = st2.enter_context(nc.sbuf_tensor("x_t", [128, DM], F32))
            xn_t = st2.enter_context(nc.sbuf_tensor("xn_t", [128, DM], BF16))
            xnf = st2.enter_context(nc.sbuf_tensor("xnf", [128, 16 * 512], BF16))
            stat = st2.enter_context(nc.sbuf_tensor("stat", [128, 8], F32))
            ps_x = st2.enter_context(nc.psum_tensor("ps_x", [128, 128], BF16))
            ps_k = st2.enter_context(nc.psum_tensor("ps_k", [128, 512], F32))
            ps_v = st2.enter_context(nc.psum_tensor("ps_v", [128, 512], F32))

            @block.gpsimd
            def _(e):
                for ci in range(16):
                    e.dma_start(out=wkb[:, ci * 512 : (ci + 1) * 512],
                                in_=wk[ci * 128 : (ci + 1) * 128, :]).then_inc(gc.h, 16)
                    gc.inc(16)
                    e.dma_start(out=wvb[:, ci * 512 : (ci + 1) * 512],
                                in_=wv[ci * 128 : (ci + 1) * 128, :]).then_inc(gc.h, 16)
                    gc.inc(16)
            g_wkv = gc.v

            @block.vector
            def _(e):
                e = AD(e)
                e.memset(vt[:], 1.0)
                e.engine_nop().then_inc(ve.h, 1)
            v_vt1 = ve.inc()

            for st8 in range(8):
                g0 = st8 * 512
                for sub in range(4):
                    row = g0 + sub * 128

                    @block.sync
                    def _(e, row=row):
                        e.wait_ge(ve.h, ve.v)
                        e.dma_start(out=x_t[:], in_=xb[row : row + 128, :]).then_inc(ld.h, 16)
                    t_x = ld.inc(16)

                    @block.scalar
                    def _(e, t_x=t_x):
                        e = AD(e)
                        e.wait_ge(ld.h, t_x)
                        e.activation(xn_t[:], x_t[:], AF.Square,
                                     accum_out=stat[:, 1:2]).then_inc(se.h, 1)
                    s_sq = se.inc()

                    @block.vector
                    def _(e, t_x=t_x, s_sq=s_sq):
                        e = AD(e)
                        e.wait_ge(ld.h, t_x)
                        e.reduce_sum(stat[:, 0:1], x_t[:], AX.X)
                        e.wait_ge(se.h, s_sq)
                        e.tensor_scalar(stat[:, 2:3], stat[:, 0:1], 1.0 / DM, None, ALU.mult)
                        e.tensor_scalar(stat[:, 3:4], stat[:, 1:2], 1.0 / DM, None, ALU.mult)
                        e.tensor_tensor(stat[:, 4:5], stat[:, 2:3], stat[:, 2:3], ALU.mult)
                        e.tensor_tensor(stat[:, 5:6], stat[:, 3:4], stat[:, 4:5], ALU.subtract)
                        e.tensor_scalar(stat[:, 5:6], stat[:, 5:6], 1e-5, None, ALU.add)
                        e.engine_nop().then_inc(ve.h, 1)
                    v_var = ve.inc()

                    @block.scalar
                    def _(e, v_var=v_var):
                        e = AD(e)
                        e.wait_ge(ve.h, v_var)
                        e.activation(stat[:, 6:7], stat[:, 5:6], AF.Sqrt).then_inc(se.h, 1)
                    s_sd = se.inc()

                    @block.vector
                    def _(e, s_sd=s_sd):
                        e = AD(e)
                        e.wait_ge(se.h, s_sd)
                        e.reciprocal(stat[:, 7:8], stat[:, 6:7])
                        e.tensor_scalar(xn_t[:], x_t[:], stat[:, 2:3], stat[:, 7:8],
                                        ALU.subtract, ALU.mult)
                        e.engine_nop().then_inc(ve.h, 1)
                    v_xn = ve.inc()

                    for ci in range(16):
                        @block.tensor
                        def _(e, ci=ci, v_xn=v_xn):
                            if ci == 0:
                                e.wait_ge(ve.h, v_xn)
                            e.wait_ge(se.h, se.v)
                            e.transpose(ps_x[:], xn_t[:, ci * 128 : (ci + 1) * 128],
                                        ident_b[:]).then_inc(pe.h, 1)
                        p_t = pe.inc()

                        @block.scalar
                        def _(e, ci=ci, sub=sub, p_t=p_t):
                            e = AD(e)
                            e.wait_ge(pe.h, p_t)
                            e.copy(xnf[:, ci * 512 + sub * 128 : ci * 512 + sub * 128 + 128],
                                   ps_x[:]).then_inc(se.h, 1)
                        se.inc()

                s_xnf = se.v
                for g in range(4):
                    @block.tensor
                    def _(e, g=g, s_xnf=s_xnf):
                        if g == 0:
                            e.wait_ge(se.h, s_xnf)
                            e.wait_ge(gc.h, g_wkv)
                        e.wait_ge(ve.h, ve.v)
                        for ci in range(16):
                            ins = e.matmul(ps_k[:], wkb[:, ci * 512 + g * 128 : ci * 512 + g * 128 + 128],
                                           xnf[:, ci * 512 : (ci + 1) * 512],
                                           start=(ci == 0), stop=(ci == 15))
                            if ci == 15:
                                ins.then_inc(pe.h, 1)
                    p_k = pe.inc()

                    @block.vector
                    def _(e, g=g, g0=g0, p_k=p_k):
                        e = AD(e)
                        e.wait_ge(pe.h, p_k)
                        e.tensor_copy(kf[:, g * S + g0 : g * S + g0 + 512], ps_k[:])
                        e.engine_nop().then_inc(ve.h, 1)
                    ve.inc()

                for sub in range(4):
                    kbt = st8 * 4 + sub

                    @block.tensor
                    def _(e, sub=sub):
                        e.wait_ge(ve.h, ve.v)
                        for ci in range(16):
                            ins = e.matmul(ps_v[:], xnf[:, ci * 512 + sub * 128 : ci * 512 + sub * 128 + 128],
                                           wvb[:, ci * 512 : (ci + 1) * 512],
                                           start=(ci == 0), stop=(ci == 15))
                            if ci == 15:
                                ins.then_inc(pe.h, 1)
                    p_v = pe.inc()

                    @block.vector
                    def _(e, kbt=kbt, p_v=p_v):
                        e = AD(e)
                        e.wait_ge(pe.h, p_v)
                        e.wait_ge(ve.h, v_vt1)
                        for g in range(4):
                            e.tensor_copy(vt[:, kbt * 516 + g * 129 : kbt * 516 + g * 129 + 128],
                                          ps_v[:, g * 128 : (g + 1) * 128])
                        e.engine_nop().then_inc(ve.h, 1)
                    ve.inc()

        if debug:
            barrier()

            @block.sync
            def _(e):
                e.dma_start(out=dbg_cos, in_=cos_t[:]).then_inc(so.h, 16)
                e.dma_start(out=dbg_sin, in_=sin_t[:]).then_inc(so.h, 16)
                e.dma_start(out=dbg_cq, in_=cos_q[:]).then_inc(so.h, 16)
                e.dma_start(out=dbg_sq, in_=sin_q[:]).then_inc(so.h, 16)
                e.dma_start(out=dbg_vt, in_=vt[:]).then_inc(so.h, 16)
                so.inc(80)

        barrier()

        with contextlib.ExitStack() as st2:
            r1 = st2.enter_context(nc.sbuf_tensor("r1", [128, 512], F32))
            r2 = st2.enter_context(nc.sbuf_tensor("r2", [128, 512], F32))
            ps_rot = st2.enter_context(nc.psum_tensor("ps_rot", [128, 512], F32))

            for g in range(4):
                for c8 in range(8):
                    lo = g * S + c8 * 512
                    ksl = kf[:, lo : lo + 512]
                    cs = cos_t[:, c8 * 512 : c8 * 512 + 512]
                    sn = sin_t[:, c8 * 512 : c8 * 512 + 512]

                    @block.tensor
                    def _(e, ksl=ksl):
                        e.wait_ge(ve.h, ve.v)
                        e.matmul(ps_rot[:], rot_m[:], ksl, start=True, stop=True).then_inc(pe.h, 1)
                    p_rot = pe.inc()

                    @block.vector
                    def _(e, ksl=ksl, cs=cs, sn=sn, p_rot=p_rot):
                        e = AD(e)
                        e.wait_ge(pe.h, p_rot)
                        e.tensor_tensor(r1[:], ksl, cs, ALU.mult)
                        e.tensor_tensor(r2[:], ps_rot[:], sn, ALU.mult)
                        e.tensor_tensor(ksl, r1[:], r2[:], ALU.add)
                        e.engine_nop().then_inc(ve.h, 1)
                    ve.inc()
        v_krope = ve.v

        for grp in range(2):
            barrier()
            with contextlib.ExitStack() as st2:
                wqb = st2.enter_context(nc.sbuf_tensor(uname("wqb"), [128, 16 * 1024], BF16))
                qfg = st2.enter_context(nc.sbuf_tensor(uname("qfg"), [128, 8 * TPC], BF16))
                x_t = st2.enter_context(nc.sbuf_tensor(uname("x_t2"), [128, DM], F32))
                xn_t = st2.enter_context(nc.sbuf_tensor(uname("xn_t2"), [128, DM], BF16))
                xnf = st2.enter_context(nc.sbuf_tensor(uname("xnf2"), [128, 16 * 512], BF16))
                stat = st2.enter_context(nc.sbuf_tensor(uname("stat2"), [128, 8], F32))
                r1 = st2.enter_context(nc.sbuf_tensor(uname("r1b"), [128, 512], F32))
                r2 = st2.enter_context(nc.sbuf_tensor(uname("r2b"), [128, 512], F32))

                @block.gpsimd
                def _(e, grp=grp):
                    for ci in range(16):
                        e.dma_start(out=wqb[:, ci * 1024 : (ci + 1) * 1024],
                                    in_=wq[ci * 128 : (ci + 1) * 128,
                                           grp * 1024 : (grp + 1) * 1024]).then_inc(gc.h, 16)
                        gc.inc(16)
                g_wq = gc.v

                for ot in range(4):
                    with contextlib.ExitStack() as st3:
                        ps_x = st3.enter_context(nc.psum_tensor(uname("ps_xq"), [128, 128], BF16))
                        for sub in range(4):
                            row = ot * 512 + sub * 128

                            @block.sync
                            def _(e, row=row):
                                e.wait_ge(ve.h, ve.v)
                                e.dma_start(out=x_t[:], in_=x_own[row : row + 128, :]).then_inc(ld.h, 16)
                            t_x = ld.inc(16)

                            @block.scalar
                            def _(e, t_x=t_x):
                                e = AD(e)
                                e.wait_ge(ld.h, t_x)
                                e.activation(xn_t[:], x_t[:], AF.Square,
                                             accum_out=stat[:, 1:2]).then_inc(se.h, 1)
                            s_sq = se.inc()

                            @block.vector
                            def _(e, t_x=t_x, s_sq=s_sq):
                                e = AD(e)
                                e.wait_ge(ld.h, t_x)
                                e.reduce_sum(stat[:, 0:1], x_t[:], AX.X)
                                e.wait_ge(se.h, s_sq)
                                e.tensor_scalar(stat[:, 2:3], stat[:, 0:1], 1.0 / DM, None, ALU.mult)
                                e.tensor_scalar(stat[:, 3:4], stat[:, 1:2], 1.0 / DM, None, ALU.mult)
                                e.tensor_tensor(stat[:, 4:5], stat[:, 2:3], stat[:, 2:3], ALU.mult)
                                e.tensor_tensor(stat[:, 5:6], stat[:, 3:4], stat[:, 4:5], ALU.subtract)
                                e.tensor_scalar(stat[:, 5:6], stat[:, 5:6], 1e-5, None, ALU.add)
                                e.engine_nop().then_inc(ve.h, 1)
                            v_var = ve.inc()

                            @block.scalar
                            def _(e, v_var=v_var):
                                e = AD(e)
                                e.wait_ge(ve.h, v_var)
                                e.activation(stat[:, 6:7], stat[:, 5:6], AF.Sqrt).then_inc(se.h, 1)
                            s_sd = se.inc()

                            @block.vector
                            def _(e, s_sd=s_sd):
                                e = AD(e)
                                e.wait_ge(se.h, s_sd)
                                e.reciprocal(stat[:, 7:8], stat[:, 6:7])
                                e.tensor_scalar(xn_t[:], x_t[:], stat[:, 2:3], stat[:, 7:8],
                                                ALU.subtract, ALU.mult)
                                e.engine_nop().then_inc(ve.h, 1)
                            v_xn = ve.inc()

                            for ci in range(16):
                                @block.tensor
                                def _(e, ci=ci, v_xn=v_xn):
                                    if ci == 0:
                                        e.wait_ge(ve.h, v_xn)
                                    e.wait_ge(se.h, se.v)
                                    e.transpose(ps_x[:], xn_t[:, ci * 128 : (ci + 1) * 128],
                                                ident_b[:]).then_inc(pe.h, 1)
                                p_t = pe.inc()

                                @block.scalar
                                def _(e, ci=ci, sub=sub, p_t=p_t):
                                    e = AD(e)
                                    e.wait_ge(pe.h, p_t)
                                    e.copy(xnf[:, ci * 512 + sub * 128 : ci * 512 + sub * 128 + 128],
                                           ps_x[:]).then_inc(se.h, 1)
                                se.inc()
                    with contextlib.ExitStack() as st3:
                        ps_q = st3.enter_context(nc.psum_tensor(uname("ps_q"), [128, 8 * 512], F32))
                        s_xnf = se.v

                        @block.tensor
                        def _(e, s_xnf=s_xnf):
                            e.wait_ge(se.h, s_xnf)
                            e.wait_ge(gc.h, g_wq)
                            e.wait_ge(ve.h, ve.v)
                            for ci in range(16):
                                for h8 in range(8):
                                    ins = e.matmul(ps_q[:, h8 * 512 : h8 * 512 + 512],
                                                   wqb[:, ci * 1024 + h8 * 128 : ci * 1024 + h8 * 128 + 128],
                                                   xnf[:, ci * 512 : (ci + 1) * 512],
                                                   start=(ci == 0), stop=(ci == 15))
                                    if ci == 15 and h8 == 7:
                                        ins.then_inc(pe.h, 1)
                        p_q = pe.inc()

                        @block.vector
                        def _(e, ot=ot, p_q=p_q):
                            e = AD(e)
                            e.wait_ge(pe.h, p_q)
                            for h8 in range(8):
                                e.tensor_scalar(qfg[:, h8 * TPC + ot * 512 : h8 * TPC + ot * 512 + 512],
                                                ps_q[:, h8 * 512 : h8 * 512 + 512],
                                                1.0 / float(np.sqrt(128.0)), None, ALU.mult)
                            e.engine_nop().then_inc(ve.h, 1)
                        ve.inc()

                with contextlib.ExitStack() as st3:
                    ps_rq = st3.enter_context(nc.psum_tensor(uname("ps_rq"), [128, 512], F32))
                    for h8 in range(8):
                        for ot in range(4):
                            lo = h8 * TPC + ot * 512
                            qsl = qfg[:, lo : lo + 512]
                            cs = cos_q[:, ot * 512 : ot * 512 + 512]
                            sn = sin_q[:, ot * 512 : ot * 512 + 512]

                            @block.tensor
                            def _(e, qsl=qsl):
                                e.wait_ge(ve.h, ve.v)
                                e.matmul(ps_rq[:], rot_m[:], qsl, start=True, stop=True).then_inc(pe.h, 1)
                            p_rot = pe.inc()

                            @block.vector
                            def _(e, qsl=qsl, cs=cs, sn=sn, p_rot=p_rot):
                                e = AD(e)
                                e.wait_ge(pe.h, p_rot)
                                e.tensor_tensor(r1[:], qsl, cs, ALU.mult)
                                e.tensor_tensor(r2[:], ps_rq[:], sn, ALU.mult)
                                e.tensor_tensor(qsl, r1[:], r2[:], ALU.add)
                                e.engine_nop().then_inc(ve.h, 1)
                            ve.inc()

                if debug and grp == 0:
                    barrier()

                    @block.sync
                    def _(e):
                        e.dma_start(out=dbg_qf, in_=qfg[:]).then_inc(so.h, 16)
                        e.dma_start(out=dbg_kf, in_=kf[:]).then_inc(so.h, 16)
                        so.inc(32)

                barrier()

                with contextlib.ExitStack() as st3:
                    ps_s = st3.enter_context(nc.psum_tensor(uname("ps_s"), [128, 512], F32))
                    ps_o = st3.enter_context(nc.psum_tensor(uname("ps_o"), [128, 129], F32))
                    ps_tp = st3.enter_context(nc.psum_tensor(uname("ps_tp"), [128, 128], BF16))
                    p_sb = st3.enter_context(nc.sbuf_tensor(uname("p_sb"), [128, 512], BF16))
                    pt_sb = st3.enter_context(nc.sbuf_tensor(uname("pt_sb"), [128, 128], BF16))
                    nm = st3.enter_context(nc.sbuf_tensor(uname("nm"), [128, 1], F32))
                    nm_old = st3.enter_context(nc.sbuf_tensor(uname("nm_old"), [128, 1], F32))
                    nrm = st3.enter_context(nc.sbuf_tensor(uname("nrm"), [128, 1], F32))
                    alpha = st3.enter_context(nc.sbuf_tensor(uname("alpha"), [128, 1], F32))
                    l_r = st3.enter_context(nc.sbuf_tensor(uname("l_r"), [128, 1], F32))
                    o_sb = st3.enter_context(nc.sbuf_tensor(uname("o_sb"), [128, 129], F32))
                    o_tok = st3.enter_context(nc.sbuf_tensor(uname("o_tok"), [128, 1024], BF16))

                    for iq in range(16):
                        diag = iq // 2
                        jmax_u = 1 + 2 * (iq % 2)

                        @block.vector
                        def _(e, iq=iq):
                            e = AD(e)
                            e.tensor_scalar(mask_c[:], iota_i[:], thr_b[:, iq : iq + 1],
                                            -1e30, ALU.is_gt, ALU.mult)
                            e.engine_nop().then_inc(ve.h, 1)
                        ve.inc()

                        for h8 in range(8):
                            h = grp * 8 + h8
                            gkv = h // 4
                            for kb in range(diag + 1):
                                is_diag = kb == diag
                                jmax = jmax_u if is_diag else 3

                                @block.tensor
                                def _(e, h8=h8, iq=iq, kb=kb, gkv=gkv):
                                    if kb == 0:
                                        e.wait_ge(ve.h, v_krope)
                                    e.wait_ge(ve.h, ve.v)
                                    e.wait_ge(se.h, se.v)
                                    e.matmul(ps_s[:],
                                             qfg[:, h8 * TPC + iq * 128 : h8 * TPC + iq * 128 + 128],
                                             kf[:, gkv * S + kb * 512 : gkv * S + kb * 512 + 512],
                                             start=True, stop=True).then_inc(pe.h, 1)
                                p_s = pe.inc()

                                @block.vector
                                def _(e, p_s=p_s, is_diag=is_diag, kb=kb):
                                    e = AD(e)
                                    e.wait_ge(pe.h, p_s)
                                    if is_diag:
                                        e.tensor_tensor(ps_s[:], ps_s[:], mask_c[:],
                                                        ALU.add)
                                    e.reduce_max(nrm[:], ps_s[:], AX.X, negate=True)
                                    if kb == 0:
                                        e.tensor_copy(nm[:], nrm[:])
                                    else:
                                        e.tensor_copy(nm_old[:], nm[:])
                                        e.tensor_tensor(nm[:], nm[:], nrm[:], ALU.min)
                                        e.tensor_tensor(alpha[:], nm[:], nm_old[:], ALU.subtract)
                                    e.engine_nop().then_inc(ve.h, 1)
                                v_m = ve.inc()

                                @block.scalar
                                def _(e, v_m=v_m, kb=kb):
                                    e = AD(e)
                                    e.wait_ge(ve.h, v_m)
                                    if kb > 0:
                                        e.activation(alpha[:], alpha[:], AF.Exp)
                                    e.activation(p_sb[:], ps_s[:], AF.Exp, bias=nm[:]).then_inc(se.h, 1)
                                s_p = se.inc()


                                for j in range(jmax + 1):
                                    kbt = kb * 4 + j

                                    @block.tensor
                                    def _(e, j=j, s_p=s_p):
                                        if j == 0:
                                            e.wait_ge(se.h, s_p)
                                        e.wait_ge(se.h, se.v)
                                        e.transpose(ps_tp[:], p_sb[:, j * 128 : (j + 1) * 128],
                                                    ident_b[:]).then_inc(pe.h, 1)
                                    p_tp = pe.inc()

                                    @block.scalar
                                    def _(e, p_tp=p_tp):
                                        e = AD(e)
                                        e.wait_ge(pe.h, p_tp)
                                        e.copy(pt_sb[:], ps_tp[:]).then_inc(se.h, 1)
                                    s_pt = se.inc()

                                    @block.tensor
                                    def _(e, kbt=kbt, gkv=gkv, s_pt=s_pt,
                                          kb=kb, j=j, jmax=jmax):
                                        e.wait_ge(se.h, s_pt)
                                        if j == 0:
                                            e.wait_ge(ve.h, ve.v)
                                        e.matmul(ps_o[:],
                                                 pt_sb[:],
                                                 vt[:, kbt * 516 + gkv * 129 : kbt * 516 + gkv * 129 + 129],
                                                 start=(j == 0),
                                                 stop=(j == jmax)).then_inc(pe.h, 1)
                                    pe.inc()

                                p_av = pe.v

                                @block.vector
                                def _(e, kb=kb, p_av=p_av):
                                    e = AD(e)
                                    e.wait_ge(pe.h, p_av)
                                    if kb == 0:
                                        e.tensor_copy(o_sb[:], ps_o[:])
                                    else:
                                        e.scalar_tensor_tensor(o_sb[:], o_sb[:], alpha[:],
                                                               ps_o[:], ALU.mult, ALU.add)
                                    e.engine_nop().then_inc(ve.h, 1)
                                ve.inc()

                            @block.vector
                            def _(e, h8=h8, h=h):
                                e = AD(e)
                                if h8 == 0:
                                    e.wait_ge(so.h, so.v)
                                e.reciprocal(l_r[:], o_sb[:, 128:129])
                                e.tensor_scalar(o_tok[:, h8 * 128 : h8 * 128 + 128],
                                                o_sb[:, 0:128], l_r[:], hs_b[:, h : h + 1],
                                                ALU.mult, ALU.mult)
                                e.engine_nop().then_inc(ve.h, 1)
                            ve.inc()

                        v_otok = ve.v

                        @block.sync
                        def _(e, iq=iq, grp=grp, v_otok=v_otok):
                            e.wait_ge(ve.h, v_otok)
                            e.dma_start(out=o_ts[iq * 128 : iq * 128 + 128,
                                                 grp * 1024 : grp * 1024 + 1024],
                                        in_=o_tok[:]).then_inc(so.h, 16)
                        so.inc(16)

        @block.gpsimd
        def _(e):
            e.wait_ge(so.h, so.v)

    return nc





def build_f(inter):
    nc = bass.Bass("TRN2", target_bir_lowering=False)
    x_own = nc.dram_tensor("x_own", [TPC, DM], F32, kind="ExternalInput").ap()
    o_in = nc.dram_tensor("o_in", [TPC, DM], BF16, kind="ExternalInput").ap()
    wo = nc.dram_tensor("wo", [DM, DM], F32, kind="ExternalInput").ap()
    wgT = nc.dram_tensor("wgT", [DM, inter], BF16, kind="ExternalInput").ap()
    wuT = nc.dram_tensor("wuT", [DM, inter], BF16, kind="ExternalInput").ap()
    wdT = nc.dram_tensor("wdT", [inter, DM], BF16, kind="ExternalInput").ap()
    x1 = nc.dram_tensor("x1", [TPC, DM], F32, kind="ExternalOutput").ap()

    noc = inter // 128
    ogs = []
    o = 0
    while o < inter:
        w = min(512, inter - o)
        ogs.append((o, w))
        o += w

    with contextlib.ExitStack() as st:
        ident_b = st.enter_context(nc.sbuf_tensor("ident_b", [128, 128], BF16))
        ident_f = st.enter_context(nc.sbuf_tensor("ident_f", [128, 128], F32))
        iota_i = st.enter_context(nc.sbuf_tensor("iota_i", [128, 128], I32))
        wo_b = st.enter_context(nc.sbuf_tensor("wo_b", [128, 16 * DM], BF16))
        h_f = st.enter_context(nc.sbuf_tensor("h_f", [128, noc * 512], BF16))

        ld = Ctr(st.enter_context(nc.semaphore("ld")))
        gc = Ctr(st.enter_context(nc.semaphore("gc")))
        gi = Ctr(st.enter_context(nc.semaphore("gi")))
        pe = Ctr(st.enter_context(nc.semaphore("pe")))
        ve = Ctr(st.enter_context(nc.semaphore("ve")))
        se = Ctr(st.enter_context(nc.semaphore("se")))
        so = Ctr(st.enter_context(nc.semaphore("so")))
        block = st.enter_context(nc.Block())

        def barrier():
            caps = [(c, c.v) for c in (ld, gc, gi, pe, ve, se, so)]
            for engname in ("sync", "gpsimd", "vector", "scalar", "tensor"):
                @getattr(block, engname)
                def _(e):
                    for c, v in caps:
                        e.wait_ge(c.h, v)

        @block.gpsimd
        def _(e):
            e = AD(e)
            e.iota(iota_i[:], [[1, 128]], channel_multiplier=-1)
            e.engine_nop().then_inc(gi.h, 1)
            for ci in range(16):
                e.dma_start(out=wo_b[:, ci * DM : (ci + 1) * DM],
                            in_=wo[ci * 128 : (ci + 1) * 128, :]).then_inc(gc.h, 16)
                gc.inc(16)
        g_iota = gi.inc()
        g_wo = gc.v

        @block.vector
        def _(e):
            e = AD(e)
            e.wait_ge(gi.h, g_iota)
            e.tensor_scalar(ident_b[:], iota_i[:], 0.0, None, ALU.is_equal)
            e.tensor_scalar(ident_f[:], iota_i[:], 0.0, None, ALU.is_equal)
            e.engine_nop().then_inc(ve.h, 1)
        ve.inc()

        for tile in range(4):
            t0 = tile * 512
            with contextlib.ExitStack() as st2:
                x1a = st2.enter_context(nc.sbuf_tensor(uname("x1a"), [128, 4 * DM], F32))
                xn_t = st2.enter_context(nc.sbuf_tensor(uname("xn_t"), [128, DM], BF16))
                stat = st2.enter_context(nc.sbuf_tensor(uname("stat"), [128, 8], F32))

                with contextlib.ExitStack() as st3:
                    xnf = st3.enter_context(nc.sbuf_tensor(uname("xnf"), [128, 16 * 512], BF16))
                    wg_b = st3.enter_context(nc.sbuf_tensor(uname("wg_b"), [128, 16 * 512], BF16))
                    wu_b = st3.enter_context(nc.sbuf_tensor(uname("wu_b"), [128, 16 * 512], BF16))
                    gsil = st3.enter_context(nc.sbuf_tensor(uname("gsil"), [128, 512], F32))
                    st3x = st3.enter_context(contextlib.ExitStack())
                    ps_x = st3x.enter_context(nc.psum_tensor(uname("ps_x"), [128, 128], BF16))

                    for sub in range(4):
                        row = t0 + sub * 128

                        @block.sync
                        def _(e, row=row, sub=sub):
                            e.wait_ge(ve.h, ve.v)
                            e.dma_start(out=x1a[:, sub * DM : (sub + 1) * DM],
                                        in_=x_own[row : row + 128, :]).then_inc(ld.h, 16)
                        t_x = ld.inc(16)
                        xsl = x1a[:, sub * DM : (sub + 1) * DM]

                        @block.scalar
                        def _(e, t_x=t_x, xsl=xsl):
                            e = AD(e)
                            e.wait_ge(ld.h, t_x)
                            e.activation(xn_t[:], xsl, AF.Square,
                                         accum_out=stat[:, 1:2]).then_inc(se.h, 1)
                        s_sq = se.inc()

                        @block.vector
                        def _(e, t_x=t_x, s_sq=s_sq, xsl=xsl):
                            e = AD(e)
                            e.wait_ge(ld.h, t_x)
                            e.reduce_sum(stat[:, 0:1], xsl, AX.X)
                            e.wait_ge(se.h, s_sq)
                            e.tensor_scalar(stat[:, 2:3], stat[:, 0:1], 1.0 / DM, None, ALU.mult)
                            e.tensor_scalar(stat[:, 3:4], stat[:, 1:2], 1.0 / DM, None, ALU.mult)
                            e.tensor_tensor(stat[:, 4:5], stat[:, 2:3], stat[:, 2:3], ALU.mult)
                            e.tensor_tensor(stat[:, 5:6], stat[:, 3:4], stat[:, 4:5], ALU.subtract)
                            e.tensor_scalar(stat[:, 5:6], stat[:, 5:6], 1e-5, None, ALU.add)
                            e.engine_nop().then_inc(ve.h, 1)
                        v_var = ve.inc()

                        @block.scalar
                        def _(e, v_var=v_var):
                            e = AD(e)
                            e.wait_ge(ve.h, v_var)
                            e.activation(stat[:, 6:7], stat[:, 5:6], AF.Sqrt).then_inc(se.h, 1)
                        s_sd = se.inc()

                        @block.vector
                        def _(e, s_sd=s_sd, xsl=xsl):
                            e = AD(e)
                            e.wait_ge(se.h, s_sd)
                            e.reciprocal(stat[:, 7:8], stat[:, 6:7])
                            e.tensor_scalar(xn_t[:], xsl, stat[:, 2:3], stat[:, 7:8],
                                            ALU.subtract, ALU.mult)
                            e.engine_nop().then_inc(ve.h, 1)
                        v_xn = ve.inc()

                        for ci in range(16):
                            @block.tensor
                            def _(e, ci=ci, v_xn=v_xn):
                                if ci == 0:
                                    e.wait_ge(ve.h, v_xn)
                                e.wait_ge(se.h, se.v)
                                e.transpose(ps_x[:], xn_t[:, ci * 128 : (ci + 1) * 128],
                                            ident_b[:]).then_inc(pe.h, 1)
                            p_t = pe.inc()

                            @block.scalar
                            def _(e, ci=ci, sub=sub, p_t=p_t):
                                e = AD(e)
                                e.wait_ge(pe.h, p_t)
                                e.copy(xnf[:, ci * 512 + sub * 128 : ci * 512 + sub * 128 + 128],
                                       ps_x[:]).then_inc(se.h, 1)
                            se.inc()
                    s_xnf = se.v
                    st3x.close()

                    with contextlib.ExitStack() as st4:
                        ps_g = st4.enter_context(nc.psum_tensor(uname("ps_g"), [128, 4 * 512], F32))
                        ps_u = st4.enter_context(nc.psum_tensor(uname("ps_u"), [128, 4 * 512], F32))
                        for ogi, (o0, w) in enumerate(ogs):
                            nos = w // 128

                            @block.sync
                            def _(e, o0=o0, w=w):
                                e.wait_ge(pe.h, pe.v)
                                for q4 in range(4):
                                    e.dma_start(
                                        out=wg_b[:, q4 * 4 * w : (q4 + 1) * 4 * w],
                                        in_=wgT.rearrange("(hc p) o -> p hc o", p=128)[
                                            :, q4 * 4 : (q4 + 1) * 4, o0 : o0 + w]).then_inc(ld.h, 16)
                                    ld.inc(16)
                                    e.dma_start(
                                        out=wu_b[:, q4 * 4 * w : (q4 + 1) * 4 * w],
                                        in_=wuT.rearrange("(hc p) o -> p hc o", p=128)[
                                            :, q4 * 4 : (q4 + 1) * 4, o0 : o0 + w]).then_inc(ld.h, 16)
                                    ld.inc(16)
                            t_w = ld.v

                            @block.tensor
                            def _(e, o0=o0, w=w, nos=nos, t_w=t_w, ogi=ogi):
                                e.wait_ge(ld.h, t_w)
                                if ogi == 0:
                                    e.wait_ge(se.h, s_xnf)
                                e.wait_ge(ve.h, ve.v)
                                for hc in range(16):
                                    for osub in range(nos):
                                        last = hc == 15 and osub == nos - 1
                                        i1 = e.matmul(ps_g[:, osub * 512 : osub * 512 + 512],
                                                      wg_b[:, hc * w + osub * 128 : hc * w + osub * 128 + 128],
                                                      xnf[:, hc * 512 : (hc + 1) * 512],
                                                      start=(hc == 0), stop=(hc == 15))
                                        i2 = e.matmul(ps_u[:, osub * 512 : osub * 512 + 512],
                                                      wu_b[:, hc * w + osub * 128 : hc * w + osub * 128 + 128],
                                                      xnf[:, hc * 512 : (hc + 1) * 512],
                                                      start=(hc == 0), stop=(hc == 15))
                                        if last:
                                            i2.then_inc(pe.h, 1)
                            p_gu = pe.inc()

                            for osub in range(nos):
                                oc = o0 // 128 + osub

                                @block.scalar
                                def _(e, osub=osub, p_gu=p_gu):
                                    e = AD(e)
                                    e.wait_ge(pe.h, p_gu)
                                    e.wait_ge(ve.h, ve.v)
                                    e.activation(gsil[:], ps_g[:, osub * 512 : osub * 512 + 512],
                                                 AF.Silu).then_inc(se.h, 1)
                                s_sil = se.inc()

                                @block.vector
                                def _(e, oc=oc, osub=osub, s_sil=s_sil):
                                    e = AD(e)
                                    e.wait_ge(se.h, s_sil)
                                    e.tensor_tensor(h_f[:, oc * 512 : (oc + 1) * 512],
                                                    gsil[:], ps_u[:, osub * 512 : osub * 512 + 512],
                                                    ALU.mult)
                                    e.engine_nop().then_inc(ve.h, 1)
                                ve.inc()

                barrier()

                with contextlib.ExitStack() as st3:
                    o_t = st3.enter_context(nc.sbuf_tensor(uname("o_t"), [128, DM], BF16))
                    of = st3.enter_context(nc.sbuf_tensor(uname("of"), [128, 16 * 512], BF16))
                    dw_b = st3.enter_context(nc.sbuf_tensor(uname("dw_b"), [128, 2 * 1024], BF16))
                    g_sb = st3.enter_context(nc.sbuf_tensor(uname("g_sb"), [128, 8 * 512], F32))
                    s_c = st3.enter_context(nc.sbuf_tensor(uname("s_c"), [128, 512], F32))
                    sg_c = st3.enter_context(nc.sbuf_tensor(uname("sg_c"), [128, 512], F32))

                    with contextlib.ExitStack() as st4:
                        ps_x = st4.enter_context(nc.psum_tensor(uname("ps_x2"), [128, 128], BF16))
                        for sub in range(4):
                            row = t0 + sub * 128

                            @block.sync
                            def _(e, row=row):
                                e.wait_ge(pe.h, pe.v)
                                e.dma_start(out=o_t[:], in_=o_in[row : row + 128, :]).then_inc(ld.h, 16)
                            t_o = ld.inc(16)

                            for ci in range(16):
                                @block.tensor
                                def _(e, ci=ci, t_o=t_o):
                                    if ci == 0:
                                        e.wait_ge(ld.h, t_o)
                                    e.wait_ge(se.h, se.v)
                                    e.transpose(ps_x[:], o_t[:, ci * 128 : (ci + 1) * 128],
                                                ident_b[:]).then_inc(pe.h, 1)
                                p_t = pe.inc()

                                @block.scalar
                                def _(e, ci=ci, sub=sub, p_t=p_t):
                                    e = AD(e)
                                    e.wait_ge(pe.h, p_t)
                                    e.copy(of[:, ci * 512 + sub * 128 : ci * 512 + sub * 128 + 128],
                                           ps_x[:]).then_inc(se.h, 1)
                                se.inc()
                    s_of = se.v

                    for dgb in range(2):
                        with contextlib.ExitStack() as st4:
                            ps_z = st4.enter_context(nc.psum_tensor(uname("ps_z"), [128, 8 * 512], F32))
                            for j in range(noc):
                                @block.sync
                                def _(e, j=j, dgb=dgb):
                                    e.wait_ge(pe.h, pe.v)
                                    e.dma_start(out=dw_b[:, (j % 2) * 1024 : (j % 2) * 1024 + 1024],
                                                in_=wdT[j * 128 : (j + 1) * 128,
                                                        dgb * 1024 : (dgb + 1) * 1024]).then_inc(ld.h, 16)
                                t_d = ld.inc(16)

                                @block.tensor
                                def _(e, j=j, t_d=t_d):
                                    e.wait_ge(ld.h, t_d)
                                    if j == 0:
                                        e.wait_ge(se.h, se.v)
                                        e.wait_ge(ve.h, ve.v)
                                    for dc in range(8):
                                        ins = e.matmul(ps_z[:, dc * 512 : dc * 512 + 512],
                                                       dw_b[:, (j % 2) * 1024 + dc * 128 : (j % 2) * 1024 + dc * 128 + 128],
                                                       h_f[:, j * 512 : (j + 1) * 512],
                                                       start=(j == 0), stop=(j == noc - 1))
                                        if dc == 7:
                                            ins.then_inc(pe.h, 1)
                                pe.inc()
                            p_dn = pe.v

                            for dc in range(8):
                                @block.scalar
                                def _(e, dc=dc, p_dn=p_dn):
                                    e = AD(e)
                                    if dc == 0:
                                        e.wait_ge(pe.h, p_dn)
                                        e.wait_ge(ve.h, ve.v)
                                    e.activation(g_sb[:, dc * 512 : (dc + 1) * 512],
                                                 ps_z[:, dc * 512 : dc * 512 + 512],
                                                 AF.Silu).then_inc(se.h, 1)
                                se.inc()
                        s_g = se.v

                        for dg4 in range(2):
                            with contextlib.ExitStack() as st4:
                                ps_s = st4.enter_context(nc.psum_tensor(uname("ps_s4"), [128, 4 * 512], F32))
                                ps_t4 = st4.enter_context(nc.psum_tensor(uname("ps_t4"), [128, 128], F32))

                                @block.tensor
                                def _(e, dgb=dgb, dg4=dg4, s_g=s_g, s_of=s_of):
                                    e.wait_ge(se.h, max(s_g, s_of))
                                    e.wait_ge(ve.h, ve.v)
                                    dbase = dgb * 1024 + dg4 * 512
                                    for ci in range(16):
                                        for ds in range(4):
                                            ins = e.matmul(
                                                ps_s[:, ds * 512 : ds * 512 + 512],
                                                wo_b[:, ci * DM + dbase + ds * 128 : ci * DM + dbase + ds * 128 + 128],
                                                of[:, ci * 512 : (ci + 1) * 512],
                                                start=(ci == 0), stop=(ci == 15))
                                            if ci == 15 and ds == 3:
                                                ins.then_inc(pe.h, 1)
                                p_wo = pe.inc()

                                for ds in range(4):
                                    dc8 = dg4 * 4 + ds
                                    dcg = dgb * 8 + dc8

                                    @block.vector
                                    def _(e, ds=ds, dc8=dc8, p_wo=p_wo):
                                        e = AD(e)
                                        e.wait_ge(pe.h, p_wo)
                                        e.tensor_copy(s_c[:], ps_s[:, ds * 512 : ds * 512 + 512])
                                        e.tensor_tensor(sg_c[:], s_c[:],
                                                        g_sb[:, dc8 * 512 : (dc8 + 1) * 512],
                                                        ALU.mult)
                                        e.engine_nop().then_inc(ve.h, 1)
                                    v_sg = ve.inc()

                                    for tsub in range(4):
                                        @block.tensor
                                        def _(e, tsub=tsub, v_sg=v_sg):
                                            if tsub == 0:
                                                e.wait_ge(ve.h, v_sg)
                                            e.wait_ge(ve.h, ve.v)
                                            e.transpose(ps_t4[:], sg_c[:, tsub * 128 : (tsub + 1) * 128],
                                                        ident_f[:]).then_inc(pe.h, 1)
                                        p_tr = pe.inc()

                                        @block.vector
                                        def _(e, tsub=tsub, dcg=dcg, p_tr=p_tr):
                                            e = AD(e)
                                            e.wait_ge(pe.h, p_tr)
                                            sl = x1a[:, tsub * DM + dcg * 128 : tsub * DM + dcg * 128 + 128]
                                            e.tensor_tensor(sl, sl, ps_t4[:], ALU.add)
                                            e.engine_nop().then_inc(ve.h, 1)
                                        ve.inc()

                v_done = ve.v
                for sub in range(4):
                    @block.sync
                    def _(e, sub=sub, v_done=v_done):
                        e.wait_ge(ve.h, v_done)
                        e.dma_start(out=x1[t0 + sub * 128 : t0 + sub * 128 + 128, :],
                                    in_=x1a[:, sub * DM : (sub + 1) * DM]).then_inc(so.h, 16)
                    so.inc(16)
                barrier()

        @block.gpsimd
        def _(e):
            e.wait_ge(so.h, so.v)

    return nc





def build_a1():
    nc = bass.Bass("TRN2", target_bir_lowering=False)
    xb = nc.dram_tensor("xb", [S, DM], F32, kind="ExternalInput").ap()
    x_own = nc.dram_tensor("x_own", [TPC, DM], F32, kind="ExternalInput").ap()
    pos_own = nc.dram_tensor("pos_own", [TPC], I32, kind="ExternalInput").ap()
    thr = nc.dram_tensor("thr", [16], F32, kind="ExternalInput").ap()
    wqd = nc.dram_tensor("wqd", [DM, QLAT], F32, kind="ExternalInput").ap()
    wqu = nc.dram_tensor("wqu", [QLAT, DM], F32, kind="ExternalInput").ap()
    wqr = nc.dram_tensor("wqr", [QLAT, 1024], F32, kind="ExternalInput").ap()
    wkvd = nc.dram_tensor("wkvd", [DM, KVLAT], F32, kind="ExternalInput").ap()
    wku = nc.dram_tensor("wku", [KVLAT, DM], F32, kind="ExternalInput").ap()
    wvu = nc.dram_tensor("wvu", [KVLAT, DM], F32, kind="ExternalInput").ap()
    wkr = nc.dram_tensor("wkr", [DM, 64], F32, kind="ExternalInput").ap()
    o1_ts = nc.dram_tensor("o1_ts", [TPC, DM], BF16, kind="ExternalOutput").ap()

    SC = 1.0 / float(np.sqrt(192.0))

    with contextlib.ExitStack() as st:
        ident_b = st.enter_context(nc.sbuf_tensor("ident_b", [128, 128], BF16))
        iota_i = st.enter_context(nc.sbuf_tensor("iota_i", [128, 512], I32))
        mask_c = st.enter_context(nc.sbuf_tensor("mask_c", [128, 512], BF16))
        cos_t = st.enter_context(nc.sbuf_tensor("cos_t", [64, S], BF16))
        sin_t = st.enter_context(nc.sbuf_tensor("sin_t", [64, S], BF16))
        cos_q = st.enter_context(nc.sbuf_tensor("cos_q", [64, TPC], BF16))
        sin_q = st.enter_context(nc.sbuf_tensor("sin_q", [64, TPC], BF16))
        rot_m = st.enter_context(nc.sbuf_tensor("rot_m", [64, 64], BF16))
        thr_b = st.enter_context(nc.sbuf_tensor("thr_b", [128, 16], F32))
        ones1 = st.enter_context(nc.sbuf_tensor("ones1", [1, 128], F32))
        kvl = st.enter_context(nc.sbuf_tensor("kvl", [128, 4 * S], BF16))
        kr = st.enter_context(nc.sbuf_tensor("kr", [64, S], BF16))
        qlf = st.enter_context(nc.sbuf_tensor("qlf", [128, 6 * TPC], BF16))

        ld = Ctr(st.enter_context(nc.semaphore("ld")))
        gc = Ctr(st.enter_context(nc.semaphore("gc")))
        gi = Ctr(st.enter_context(nc.semaphore("gi")))
        pe = Ctr(st.enter_context(nc.semaphore("pe")))
        ve = Ctr(st.enter_context(nc.semaphore("ve")))
        se = Ctr(st.enter_context(nc.semaphore("se")))
        so = Ctr(st.enter_context(nc.semaphore("so")))
        block = st.enter_context(nc.Block())

        def barrier():
            caps = [(c, c.v) for c in (ld, gc, gi, pe, ve, se, so)]
            for engname in ("sync", "gpsimd", "vector", "scalar", "tensor"):
                @getattr(block, engname)
                def _(e):
                    for c, v in caps:
                        e.wait_ge(c.h, v)

        with contextlib.ExitStack() as st2:
            tbA = st2.enter_context(nc.sbuf_tensor("tbA", [64, S], F32))
            tbB = st2.enter_context(nc.sbuf_tensor("tbB", [64, S], F32))
            tbI = st2.enter_context(nc.sbuf_tensor("tbI", [64, S], I32))
            tbC = st2.enter_context(nc.sbuf_tensor("tbC", [64, S], F32))
            invf = st2.enter_context(nc.sbuf_tensor("invf", [64, 1], F32))
            pidx = st2.enter_context(nc.sbuf_tensor("pidx", [64, 1], I32))
            thr_row = st2.enter_context(nc.sbuf_tensor("thr_row", [1, 16], F32))
            pos_row = st2.enter_context(nc.sbuf_tensor("pos_row", [1, TPC], I32))
            pos_f = st2.enter_context(nc.sbuf_tensor("pos_f", [1, TPC], F32))
            ps_h = st2.enter_context(nc.psum_tensor("ps_h", [128, 16], F32))
            ps_pb = st2.enter_context(nc.psum_tensor("ps_pb", [64, 512], F32))

            @block.sync
            def _(e):
                e.dma_start(out=thr_row[:], in_=thr[None, :]).then_inc(ld.h, 16)
                ld.inc(16)
                e.dma_start(out=pos_row[:], in_=pos_own[None, :]).then_inc(ld.h, 16)
                ld.inc(16)
            t_pre = ld.v

            @block.gpsimd
            def _(e):
                e = AD(e)
                e.iota(iota_i[:], [[1, 512]], channel_multiplier=-1)
                e.iota(pidx[:], [[0, 1]], channel_multiplier=1)
                e.iota(tbI[:], [[1, S]], channel_multiplier=0)
                e.engine_nop().then_inc(gi.h, 1)
            g_iota = gi.inc()

            @block.vector
            def _(e):
                e = AD(e)
                e.wait_ge(gi.h, g_iota)
                e.memset(ones1[:], 1.0)
                e.tensor_scalar(ident_b[:], iota_i[:, 0:128], 0.0, None, ALU.is_equal)
                e.tensor_scalar(rot_m[:], iota_i[0:64, 0:64], 32.0, None, ALU.is_equal)
                e.tensor_scalar(tbC[:, 0:64], iota_i[0:64, 0:64], -32.0, -1.0,
                                ALU.is_equal, ALU.mult)
                e.tensor_tensor(rot_m[:], rot_m[:], tbC[:, 0:64], ALU.add)
                e.tensor_copy(invf[:], pidx[:])
                e.tensor_scalar(tbC[:, 0:1], invf[:], 31.5, 32.0, ALU.is_gt, ALU.mult)
                e.tensor_tensor(invf[:], invf[:], tbC[:, 0:1], ALU.subtract)
                e.engine_nop().then_inc(ve.h, 1)
            v_pre = ve.inc()

            @block.scalar
            def _(e):
                e = AD(e)
                e.wait_ge(ve.h, v_pre)
                e.activation(invf[:], invf[:], AF.Exp, scale=-LOG1E4 / 32.0).then_inc(se.h, 1)
            s_invf = se.inc()

            def emit_phase(e, m_ap, dst, w):
                e.tensor_copy(tbI[:, 0:w], m_ap)
                e.tensor_copy(dst, tbI[:, 0:w])
                e.tensor_tensor(dst, m_ap, dst, ALU.subtract)
                e.tensor_scalar(tbC[:, 0:w], dst, 0.5, None, ALU.is_ge)
                e.tensor_tensor(dst, dst, tbC[:, 0:w], ALU.subtract)

            @block.vector
            def _(e):
                e = AD(e)
                e.wait_ge(se.h, s_invf)
                e.tensor_copy(tbA[:], tbI[:])
                e.tensor_scalar(tbA[:], tbA[:], invf[:], None, ALU.mult)
                e.tensor_scalar(tbB[:], tbA[:], 1.0 / TWO_PI, None, ALU.mult)
                emit_phase(e, tbB[:], tbA[:], S)
                e.engine_nop().then_inc(ve.h, 1)
            v_ph = ve.inc()

            @block.scalar
            def _(e):
                e = AD(e)
                e.wait_ge(ve.h, v_ph)
                e.activation(sin_t[:], tbA[:], AF.Sin, scale=TWO_PI).then_inc(se.h, 1)
            se.inc()

            @block.vector
            def _(e):
                e = AD(e)
                e.wait_ge(se.h, se.v)
                e.tensor_scalar(tbB[:], tbB[:], 0.25, None, ALU.add)
                emit_phase(e, tbB[:], tbA[:], S)
                e.engine_nop().then_inc(ve.h, 1)
            v_phc = ve.inc()

            @block.scalar
            def _(e):
                e = AD(e)
                e.wait_ge(ve.h, v_phc)
                e.activation(cos_t[:], tbA[:], AF.Sin, scale=TWO_PI).then_inc(se.h, 1)
            se.inc()

            @block.vector
            def _(e):
                e = AD(e)
                e.wait_ge(ld.h, t_pre)
                e.tensor_copy(pos_f[:], pos_row[:]).then_inc(ve.h, 1)
            v_posf = ve.inc()

            @block.tensor
            def _(e):
                e.wait_ge(ve.h, v_posf)
                e.matmul(ps_h[:], ones1[:], thr_row[:], start=True, stop=True).then_inc(pe.h, 1)
            p_thr = pe.inc()

            @block.vector
            def _(e):
                e = AD(e)
                e.wait_ge(pe.h, p_thr)
                e.tensor_copy(thr_b[:], ps_h[:]).then_inc(ve.h, 1)
            ve.inc()

            for c4 in range(4):
                @block.tensor
                def _(e, c4=c4):
                    e.wait_ge(ve.h, ve.v)
                    e.matmul(ps_pb[:], ones1[0:1, 0:64],
                             pos_f[:, c4 * 512 : (c4 + 1) * 512],
                             start=True, stop=True).then_inc(pe.h, 1)
                p_pb = pe.inc()

                @block.vector
                def _(e, c4=c4, p_pb=p_pb):
                    e = AD(e)
                    e.wait_ge(pe.h, p_pb)
                    e.tensor_copy(tbB[:, c4 * 512 : (c4 + 1) * 512], ps_pb[:]).then_inc(ve.h, 1)
                ve.inc()
            v_posb = ve.v

            @block.vector
            def _(e):
                e = AD(e)
                e.wait_ge(ve.h, v_posb)
                e.wait_ge(se.h, se.v)
                e.tensor_scalar(tbA[:, 0:TPC], tbB[:, 0:TPC], invf[:], None, ALU.mult)
                e.tensor_scalar(tbB[:, 0:TPC], tbA[:, 0:TPC], 1.0 / TWO_PI, None, ALU.mult)
                emit_phase(e, tbB[:, 0:TPC], tbA[:, 0:TPC], TPC)
                e.engine_nop().then_inc(ve.h, 1)
            v_qph = ve.inc()

            @block.scalar
            def _(e):
                e = AD(e)
                e.wait_ge(ve.h, v_qph)
                e.activation(sin_q[:], tbA[:, 0:TPC], AF.Sin, scale=TWO_PI).then_inc(se.h, 1)
            se.inc()

            @block.vector
            def _(e):
                e = AD(e)
                e.wait_ge(se.h, se.v)
                e.tensor_scalar(tbB[:, 0:TPC], tbB[:, 0:TPC], 0.25, None, ALU.add)
                emit_phase(e, tbB[:, 0:TPC], tbA[:, 0:TPC], TPC)
                e.engine_nop().then_inc(ve.h, 1)
            v_qphc = ve.inc()

            @block.scalar
            def _(e):
                e = AD(e)
                e.wait_ge(ve.h, v_qphc)
                e.activation(cos_q[:], tbA[:, 0:TPC], AF.Sin, scale=TWO_PI).then_inc(se.h, 1)
            se.inc()

        barrier()

        with contextlib.ExitStack() as st2:
            wkvb = st2.enter_context(nc.sbuf_tensor("wkvb", [128, 16 * 512], BF16))
            wqdb = st2.enter_context(nc.sbuf_tensor("wqdb", [128, 16 * 768], BF16))
            wkrb = st2.enter_context(nc.sbuf_tensor("wkrb", [128, 16 * 64], BF16))
            x_t = st2.enter_context(nc.sbuf_tensor("x_t", [128, DM], F32))
            xn_t = st2.enter_context(nc.sbuf_tensor("xn_t", [128, DM], BF16))
            xnf = st2.enter_context(nc.sbuf_tensor("xnf", [128, 16 * 512], BF16))
            stat = st2.enter_context(nc.sbuf_tensor("stat", [128, 8], F32))

            @block.gpsimd
            def _(e):
                for ci in range(16):
                    e.dma_start(out=wkvb[:, ci * 512 : (ci + 1) * 512],
                                in_=wkvd[ci * 128 : (ci + 1) * 128, :]).then_inc(gc.h, 16)
                    gc.inc(16)
                    e.dma_start(out=wqdb[:, ci * 768 : (ci + 1) * 768],
                                in_=wqd[ci * 128 : (ci + 1) * 128, :]).then_inc(gc.h, 16)
                    gc.inc(16)
                    e.dma_start(out=wkrb[:, ci * 64 : (ci + 1) * 64],
                                in_=wkr[ci * 128 : (ci + 1) * 128, :]).then_inc(gc.h, 16)
                    gc.inc(16)
            g_w = gc.v

            for st8 in range(8):
                g0 = st8 * 512
                own_tile = None
                with contextlib.ExitStack() as st3:
                    ps_x = st3.enter_context(nc.psum_tensor(uname("ps_x"), [128, 128], BF16))
                    ps_a = st3.enter_context(nc.psum_tensor(uname("ps_a"), [128, 512], F32))

                    for sub in range(4):
                        row = g0 + sub * 128

                        @block.sync
                        def _(e, row=row):
                            e.wait_ge(ve.h, ve.v)
                            e.dma_start(out=x_t[:], in_=xb[row : row + 128, :]).then_inc(ld.h, 16)
                        t_x = ld.inc(16)

                        @block.scalar
                        def _(e, t_x=t_x):
                            e = AD(e)
                            e.wait_ge(ld.h, t_x)
                            e.activation(xn_t[:], x_t[:], AF.Square,
                                         accum_out=stat[:, 1:2]).then_inc(se.h, 1)
                        s_sq = se.inc()

                        @block.vector
                        def _(e, t_x=t_x, s_sq=s_sq):
                            e = AD(e)
                            e.wait_ge(ld.h, t_x)
                            e.reduce_sum(stat[:, 0:1], x_t[:], AX.X)
                            e.wait_ge(se.h, s_sq)
                            e.tensor_scalar(stat[:, 2:3], stat[:, 0:1], 1.0 / DM, None, ALU.mult)
                            e.tensor_scalar(stat[:, 3:4], stat[:, 1:2], 1.0 / DM, None, ALU.mult)
                            e.tensor_tensor(stat[:, 4:5], stat[:, 2:3], stat[:, 2:3], ALU.mult)
                            e.tensor_tensor(stat[:, 5:6], stat[:, 3:4], stat[:, 4:5], ALU.subtract)
                            e.tensor_scalar(stat[:, 5:6], stat[:, 5:6], 1e-5, None, ALU.add)
                            e.engine_nop().then_inc(ve.h, 1)
                        v_var = ve.inc()

                        @block.scalar
                        def _(e, v_var=v_var):
                            e = AD(e)
                            e.wait_ge(ve.h, v_var)
                            e.activation(stat[:, 6:7], stat[:, 5:6], AF.Sqrt).then_inc(se.h, 1)
                        s_sd = se.inc()

                        @block.vector
                        def _(e, s_sd=s_sd):
                            e = AD(e)
                            e.wait_ge(se.h, s_sd)
                            e.reciprocal(stat[:, 7:8], stat[:, 6:7])
                            e.tensor_scalar(xn_t[:], x_t[:], stat[:, 2:3], stat[:, 7:8],
                                            ALU.subtract, ALU.mult)
                            e.engine_nop().then_inc(ve.h, 1)
                        v_xn = ve.inc()

                        for ci in range(16):
                            @block.tensor
                            def _(e, ci=ci, v_xn=v_xn):
                                if ci == 0:
                                    e.wait_ge(ve.h, v_xn)
                                e.wait_ge(se.h, se.v)
                                e.transpose(ps_x[:], xn_t[:, ci * 128 : (ci + 1) * 128],
                                            ident_b[:]).then_inc(pe.h, 1)
                            p_t = pe.inc()

                            @block.scalar
                            def _(e, ci=ci, sub=sub, p_t=p_t):
                                e = AD(e)
                                e.wait_ge(pe.h, p_t)
                                e.copy(xnf[:, ci * 512 + sub * 128 : ci * 512 + sub * 128 + 128],
                                       ps_x[:]).then_inc(se.h, 1)
                            se.inc()
                    s_xnf = se.v

                    for c in range(4):
                        @block.tensor
                        def _(e, c=c, s_xnf=s_xnf):
                            if c == 0:
                                e.wait_ge(se.h, s_xnf)
                                e.wait_ge(gc.h, g_w)
                            e.wait_ge(ve.h, ve.v)
                            for ci in range(16):
                                ins = e.matmul(ps_a[:],
                                               wkvb[:, ci * 512 + c * 128 : ci * 512 + c * 128 + 128],
                                               xnf[:, ci * 512 : (ci + 1) * 512],
                                               start=(ci == 0), stop=(ci == 15))
                                if ci == 15:
                                    ins.then_inc(pe.h, 1)
                        p_c = pe.inc()

                        @block.vector
                        def _(e, c=c, g0=g0, p_c=p_c):
                            e = AD(e)
                            e.wait_ge(pe.h, p_c)
                            e.tensor_copy(kvl[:, c * S + g0 : c * S + g0 + 512], ps_a[:])
                            e.engine_nop().then_inc(ve.h, 1)
                        ve.inc()

                    @block.tensor
                    def _(e):
                        e.wait_ge(ve.h, ve.v)
                        for ci in range(16):
                            ins = e.matmul(ps_a[0:64, :],
                                           wkrb[:, ci * 64 : (ci + 1) * 64],
                                           xnf[:, ci * 512 : (ci + 1) * 512],
                                           start=(ci == 0), stop=(ci == 15))
                            if ci == 15:
                                ins.then_inc(pe.h, 1)
                    p_kr = pe.inc()

                    @block.vector
                    def _(e, g0=g0, p_kr=p_kr):
                        e = AD(e)
                        e.wait_ge(pe.h, p_kr)
                        e.tensor_copy(kr[:, g0 : g0 + 512], ps_a[0:64, :])
                        e.engine_nop().then_inc(ve.h, 1)
                    ve.inc()

            for ot in range(4):
                t0 = ot * 512
                with contextlib.ExitStack() as st3:
                    ps_x = st3.enter_context(nc.psum_tensor(uname("ps_x"), [128, 128], BF16))
                    ps_a = st3.enter_context(nc.psum_tensor(uname("ps_a"), [128, 512], F32))

                    for sub in range(4):
                        row = t0 + sub * 128

                        @block.sync
                        def _(e, row=row):
                            e.wait_ge(ve.h, ve.v)
                            e.dma_start(out=x_t[:], in_=x_own[row : row + 128, :]).then_inc(ld.h, 16)
                        t_x = ld.inc(16)

                        @block.scalar
                        def _(e, t_x=t_x):
                            e = AD(e)
                            e.wait_ge(ld.h, t_x)
                            e.activation(xn_t[:], x_t[:], AF.Square,
                                         accum_out=stat[:, 1:2]).then_inc(se.h, 1)
                        s_sq = se.inc()

                        @block.vector
                        def _(e, t_x=t_x, s_sq=s_sq):
                            e = AD(e)
                            e.wait_ge(ld.h, t_x)
                            e.reduce_sum(stat[:, 0:1], x_t[:], AX.X)
                            e.wait_ge(se.h, s_sq)
                            e.tensor_scalar(stat[:, 2:3], stat[:, 0:1], 1.0 / DM, None, ALU.mult)
                            e.tensor_scalar(stat[:, 3:4], stat[:, 1:2], 1.0 / DM, None, ALU.mult)
                            e.tensor_tensor(stat[:, 4:5], stat[:, 2:3], stat[:, 2:3], ALU.mult)
                            e.tensor_tensor(stat[:, 5:6], stat[:, 3:4], stat[:, 4:5], ALU.subtract)
                            e.tensor_scalar(stat[:, 5:6], stat[:, 5:6], 1e-5, None, ALU.add)
                            e.engine_nop().then_inc(ve.h, 1)
                        v_var = ve.inc()

                        @block.scalar
                        def _(e, v_var=v_var):
                            e = AD(e)
                            e.wait_ge(ve.h, v_var)
                            e.activation(stat[:, 6:7], stat[:, 5:6], AF.Sqrt).then_inc(se.h, 1)
                        s_sd = se.inc()

                        @block.vector
                        def _(e, s_sd=s_sd):
                            e = AD(e)
                            e.wait_ge(se.h, s_sd)
                            e.reciprocal(stat[:, 7:8], stat[:, 6:7])
                            e.tensor_scalar(xn_t[:], x_t[:], stat[:, 2:3], stat[:, 7:8],
                                            ALU.subtract, ALU.mult)
                            e.engine_nop().then_inc(ve.h, 1)
                        v_xn = ve.inc()

                        for ci in range(16):
                            @block.tensor
                            def _(e, ci=ci, v_xn=v_xn):
                                if ci == 0:
                                    e.wait_ge(ve.h, v_xn)
                                e.wait_ge(se.h, se.v)
                                e.transpose(ps_x[:], xn_t[:, ci * 128 : (ci + 1) * 128],
                                            ident_b[:]).then_inc(pe.h, 1)
                            p_t = pe.inc()

                            @block.scalar
                            def _(e, ci=ci, sub=sub, p_t=p_t):
                                e = AD(e)
                                e.wait_ge(pe.h, p_t)
                                e.copy(xnf[:, ci * 512 + sub * 128 : ci * 512 + sub * 128 + 128],
                                       ps_x[:]).then_inc(se.h, 1)
                            se.inc()
                    s_xnf = se.v

                    for c in range(6):
                        @block.tensor
                        def _(e, c=c, s_xnf=s_xnf):
                            if c == 0:
                                e.wait_ge(se.h, s_xnf)
                            e.wait_ge(ve.h, ve.v)
                            for ci in range(16):
                                ins = e.matmul(ps_a[:],
                                               wqdb[:, ci * 768 + c * 128 : ci * 768 + c * 128 + 128],
                                               xnf[:, ci * 512 : (ci + 1) * 512],
                                               start=(ci == 0), stop=(ci == 15))
                                if ci == 15:
                                    ins.then_inc(pe.h, 1)
                        p_c = pe.inc()

                        @block.vector
                        def _(e, c=c, t0=t0, p_c=p_c):
                            e = AD(e)
                            e.wait_ge(pe.h, p_c)
                            e.tensor_copy(qlf[:, c * TPC + t0 : c * TPC + t0 + 512], ps_a[:])
                            e.engine_nop().then_inc(ve.h, 1)
                        ve.inc()

        barrier()

        with contextlib.ExitStack() as st2:
            r1 = st2.enter_context(nc.sbuf_tensor("r1k", [64, 512], F32))
            r2 = st2.enter_context(nc.sbuf_tensor("r2k", [64, 512], F32))
            ps_rot = st2.enter_context(nc.psum_tensor("ps_rotk", [64, 512], F32))

            for c8 in range(8):
                ksl = kr[:, c8 * 512 : c8 * 512 + 512]
                cs = cos_t[:, c8 * 512 : c8 * 512 + 512]
                sn = sin_t[:, c8 * 512 : c8 * 512 + 512]

                @block.tensor
                def _(e, ksl=ksl):
                    e.wait_ge(ve.h, ve.v)
                    e.matmul(ps_rot[:], rot_m[:], ksl, start=True, stop=True).then_inc(pe.h, 1)
                p_rot = pe.inc()

                @block.vector
                def _(e, ksl=ksl, cs=cs, sn=sn, p_rot=p_rot):
                    e = AD(e)
                    e.wait_ge(pe.h, p_rot)
                    e.tensor_tensor(r1[:], ksl, cs, ALU.mult)
                    e.tensor_tensor(r2[:], ps_rot[:], sn, ALU.mult)
                    e.tensor_tensor(ksl, r1[:], r2[:], ALU.add)
                    e.engine_nop().then_inc(ve.h, 1)
                ve.inc()
        v_krope = ve.v

        for h in range(16):
            barrier()
            with contextlib.ExitStack() as st2:
                wku_h = st2.enter_context(nc.sbuf_tensor(uname("wku_h"), [128, 4 * 128], BF16))
                wvu_h = st2.enter_context(nc.sbuf_tensor(uname("wvu_h"), [128, 4 * 128], BF16))
                wqu_h = st2.enter_context(nc.sbuf_tensor(uname("wqu_h"), [128, 6 * 128], BF16))
                wqr_h = st2.enter_context(nc.sbuf_tensor(uname("wqr_h"), [128, 6 * 64], BF16))
                kc_h = st2.enter_context(nc.sbuf_tensor(uname("kc_h"), [128, S], BF16))
                vt_h = st2.enter_context(nc.sbuf_tensor(uname("vt_h"), [128, 32 * 129], BF16))
                qc_h = st2.enter_context(nc.sbuf_tensor(uname("qc_h"), [128, TPC], BF16))
                qr_h = st2.enter_context(nc.sbuf_tensor(uname("qr_h"), [64, TPC], BF16))

                @block.gpsimd
                def _(e, h=h):
                    for c in range(4):
                        e.dma_start(out=wku_h[:, c * 128 : (c + 1) * 128],
                                    in_=wku[c * 128 : (c + 1) * 128,
                                            h * 128 : (h + 1) * 128]).then_inc(gc.h, 16)
                        gc.inc(16)
                        e.dma_start(out=wvu_h[:, c * 128 : (c + 1) * 128],
                                    in_=wvu[c * 128 : (c + 1) * 128,
                                            h * 128 : (h + 1) * 128]).then_inc(gc.h, 16)
                        gc.inc(16)
                    for c in range(6):
                        e.dma_start(out=wqu_h[:, c * 128 : (c + 1) * 128],
                                    in_=wqu[c * 128 : (c + 1) * 128,
                                            h * 128 : (h + 1) * 128]).then_inc(gc.h, 16)
                        gc.inc(16)
                        e.dma_start(out=wqr_h[:, c * 64 : (c + 1) * 64],
                                    in_=wqr[c * 128 : (c + 1) * 128,
                                            h * 64 : (h + 1) * 64]).then_inc(gc.h, 16)
                        gc.inc(16)
                g_wh = gc.v

                @block.vector
                def _(e):
                    e = AD(e)
                    e.memset(vt_h[:], 1.0)
                    e.engine_nop().then_inc(ve.h, 1)
                v_vt1 = ve.inc()

                with contextlib.ExitStack() as st3:
                    ps_a = st3.enter_context(nc.psum_tensor(uname("ps_ah"), [128, 512], F32))
                    ps_v = st3.enter_context(nc.psum_tensor(uname("ps_vh"), [128, 128], F32))
                    rq1 = st3.enter_context(nc.sbuf_tensor(uname("rq1"), [64, 512], F32))
                    rq2 = st3.enter_context(nc.sbuf_tensor(uname("rq2"), [64, 512], F32))

                    for kb8 in range(8):
                        @block.tensor
                        def _(e, kb8=kb8, g_wh=g_wh):
                            if kb8 == 0:
                                e.wait_ge(gc.h, g_wh)
                                e.wait_ge(ve.h, v_krope)
                            e.wait_ge(ve.h, ve.v)
                            for c in range(4):
                                ins = e.matmul(ps_a[:],
                                               wku_h[:, c * 128 : (c + 1) * 128],
                                               kvl[:, c * S + kb8 * 512 : c * S + kb8 * 512 + 512],
                                               start=(c == 0), stop=(c == 3))
                                if c == 3:
                                    ins.then_inc(pe.h, 1)
                        p_kc = pe.inc()

                        @block.vector
                        def _(e, kb8=kb8, p_kc=p_kc):
                            e = AD(e)
                            e.wait_ge(pe.h, p_kc)
                            e.tensor_copy(kc_h[:, kb8 * 512 : kb8 * 512 + 512], ps_a[:])
                            e.engine_nop().then_inc(ve.h, 1)
                        ve.inc()

                    for kbt in range(32):
                        @block.tensor
                        def _(e, kbt=kbt):
                            e.wait_ge(ve.h, ve.v)
                            for c in range(4):
                                ins = e.matmul(ps_v[:],
                                               kvl[:, c * S + kbt * 128 : c * S + kbt * 128 + 128],
                                               wvu_h[:, c * 128 : (c + 1) * 128],
                                               start=(c == 0), stop=(c == 3))
                                if c == 3:
                                    ins.then_inc(pe.h, 1)
                        p_v = pe.inc()

                        @block.vector
                        def _(e, kbt=kbt, p_v=p_v):
                            e = AD(e)
                            e.wait_ge(pe.h, p_v)
                            e.wait_ge(ve.h, v_vt1)
                            e.tensor_copy(vt_h[:, kbt * 129 : kbt * 129 + 128], ps_v[:])
                            e.engine_nop().then_inc(ve.h, 1)
                        ve.inc()

                    for ot in range(4):
                        @block.tensor
                        def _(e, ot=ot):
                            e.wait_ge(ve.h, ve.v)
                            for c in range(6):
                                ins = e.matmul(ps_a[:],
                                               wqu_h[:, c * 128 : (c + 1) * 128],
                                               qlf[:, c * TPC + ot * 512 : c * TPC + ot * 512 + 512],
                                               start=(c == 0), stop=(c == 5))
                                if c == 5:
                                    ins.then_inc(pe.h, 1)
                        p_qc = pe.inc()

                        @block.vector
                        def _(e, ot=ot, p_qc=p_qc):
                            e = AD(e)
                            e.wait_ge(pe.h, p_qc)
                            e.tensor_scalar(qc_h[:, ot * 512 : ot * 512 + 512], ps_a[:],
                                            SC, None, ALU.mult)
                            e.engine_nop().then_inc(ve.h, 1)
                        ve.inc()

                        @block.tensor
                        def _(e, ot=ot):
                            e.wait_ge(ve.h, ve.v)
                            for c in range(6):
                                ins = e.matmul(ps_a[0:64, :],
                                               wqr_h[:, c * 64 : (c + 1) * 64],
                                               qlf[:, c * TPC + ot * 512 : c * TPC + ot * 512 + 512],
                                               start=(c == 0), stop=(c == 5))
                                if c == 5:
                                    ins.then_inc(pe.h, 1)
                        p_qr = pe.inc()

                        @block.vector
                        def _(e, ot=ot, p_qr=p_qr):
                            e = AD(e)
                            e.wait_ge(pe.h, p_qr)
                            e.tensor_scalar(qr_h[:, ot * 512 : ot * 512 + 512], ps_a[0:64, :],
                                            SC, None, ALU.mult)
                            e.engine_nop().then_inc(ve.h, 1)
                        ve.inc()

                    for ot in range(4):
                        qsl = qr_h[:, ot * 512 : ot * 512 + 512]
                        cs = cos_q[:, ot * 512 : ot * 512 + 512]
                        sn = sin_q[:, ot * 512 : ot * 512 + 512]

                        @block.tensor
                        def _(e, qsl=qsl):
                            e.wait_ge(ve.h, ve.v)
                            e.matmul(ps_a[0:64, :], rot_m[:], qsl, start=True, stop=True).then_inc(pe.h, 1)
                        p_rot = pe.inc()

                        @block.vector
                        def _(e, qsl=qsl, cs=cs, sn=sn, p_rot=p_rot):
                            e = AD(e)
                            e.wait_ge(pe.h, p_rot)
                            e.tensor_tensor(rq1[:], qsl, cs, ALU.mult)
                            e.tensor_tensor(rq2[:], ps_a[0:64, :], sn, ALU.mult)
                            e.tensor_tensor(qsl, rq1[:], rq2[:], ALU.add)
                            e.engine_nop().then_inc(ve.h, 1)
                        ve.inc()

                with contextlib.ExitStack() as st3:
                    ps_s = st3.enter_context(nc.psum_tensor(uname("ps_s1"), [128, 512], F32))
                    ps_o = st3.enter_context(nc.psum_tensor(uname("ps_o1"), [128, 129], F32))
                    ps_tp = st3.enter_context(nc.psum_tensor(uname("ps_tp1"), [128, 128], BF16))
                    p_sb = st3.enter_context(nc.sbuf_tensor(uname("p_sb1"), [128, 512], BF16))
                    pt_sb = st3.enter_context(nc.sbuf_tensor(uname("pt_sb1"), [128, 128], BF16))
                    nm = st3.enter_context(nc.sbuf_tensor(uname("nm1"), [128, 1], F32))
                    nm_old = st3.enter_context(nc.sbuf_tensor(uname("nmo1"), [128, 1], F32))
                    nrm = st3.enter_context(nc.sbuf_tensor(uname("nrm1"), [128, 1], F32))
                    alpha = st3.enter_context(nc.sbuf_tensor(uname("alpha1"), [128, 1], F32))
                    l_r = st3.enter_context(nc.sbuf_tensor(uname("l_r1"), [128, 1], F32))
                    o_sb = st3.enter_context(nc.sbuf_tensor(uname("o_sb1"), [128, 129], F32))
                    o_st = st3.enter_context(nc.sbuf_tensor(uname("o_st1"), [128, 128], BF16))

                    for iq in range(16):
                        diag = iq // 2
                        jmax_u = 1 + 2 * (iq % 2)

                        @block.vector
                        def _(e, iq=iq):
                            e = AD(e)
                            e.tensor_scalar(mask_c[:], iota_i[:], thr_b[:, iq : iq + 1],
                                            -1e30, ALU.is_gt, ALU.mult)
                            e.engine_nop().then_inc(ve.h, 1)
                        ve.inc()

                        for kb in range(diag + 1):
                            is_diag = kb == diag
                            jmax = jmax_u if is_diag else 3

                            @block.tensor
                            def _(e, iq=iq, kb=kb):
                                e.wait_ge(ve.h, ve.v)
                                e.wait_ge(se.h, se.v)
                                e.matmul(ps_s[:],
                                         qc_h[:, iq * 128 : iq * 128 + 128],
                                         kc_h[:, kb * 512 : kb * 512 + 512],
                                         start=True, stop=False)
                                e.matmul(ps_s[:],
                                         qr_h[:, iq * 128 : iq * 128 + 128],
                                         kr[:, kb * 512 : kb * 512 + 512],
                                         start=False, stop=True).then_inc(pe.h, 1)
                            p_s = pe.inc()

                            @block.vector
                            def _(e, p_s=p_s, is_diag=is_diag, kb=kb):
                                e = AD(e)
                                e.wait_ge(pe.h, p_s)
                                if is_diag:
                                    e.tensor_tensor(ps_s[:], ps_s[:], mask_c[:], ALU.add)
                                e.reduce_max(nrm[:], ps_s[:], AX.X, negate=True)
                                if kb == 0:
                                    e.tensor_copy(nm[:], nrm[:])
                                else:
                                    e.tensor_copy(nm_old[:], nm[:])
                                    e.tensor_tensor(nm[:], nm[:], nrm[:], ALU.min)
                                    e.tensor_tensor(alpha[:], nm[:], nm_old[:], ALU.subtract)
                                e.engine_nop().then_inc(ve.h, 1)
                            v_m = ve.inc()

                            @block.scalar
                            def _(e, v_m=v_m, kb=kb):
                                e = AD(e)
                                e.wait_ge(ve.h, v_m)
                                if kb > 0:
                                    e.activation(alpha[:], alpha[:], AF.Exp)
                                e.activation(p_sb[:], ps_s[:], AF.Exp, bias=nm[:]).then_inc(se.h, 1)
                            s_p = se.inc()

                            for j in range(jmax + 1):
                                kbt = kb * 4 + j

                                @block.tensor
                                def _(e, j=j, s_p=s_p):
                                    if j == 0:
                                        e.wait_ge(se.h, s_p)
                                    e.wait_ge(se.h, se.v)
                                    e.transpose(ps_tp[:], p_sb[:, j * 128 : (j + 1) * 128],
                                                ident_b[:]).then_inc(pe.h, 1)
                                p_tp = pe.inc()

                                @block.scalar
                                def _(e, p_tp=p_tp):
                                    e = AD(e)
                                    e.wait_ge(pe.h, p_tp)
                                    e.copy(pt_sb[:], ps_tp[:]).then_inc(se.h, 1)
                                s_pt = se.inc()

                                @block.tensor
                                def _(e, kbt=kbt, s_pt=s_pt, j=j, jmax=jmax):
                                    e.wait_ge(se.h, s_pt)
                                    if j == 0:
                                        e.wait_ge(ve.h, ve.v)
                                    e.matmul(ps_o[:],
                                             pt_sb[:],
                                             vt_h[:, kbt * 129 : kbt * 129 + 129],
                                             start=(j == 0), stop=(j == jmax)).then_inc(pe.h, 1)
                                pe.inc()

                            p_av = pe.v

                            @block.vector
                            def _(e, kb=kb, p_av=p_av):
                                e = AD(e)
                                e.wait_ge(pe.h, p_av)
                                if kb == 0:
                                    e.tensor_copy(o_sb[:], ps_o[:])
                                else:
                                    e.scalar_tensor_tensor(o_sb[:], o_sb[:], alpha[:],
                                                           ps_o[:], ALU.mult, ALU.add)
                                e.engine_nop().then_inc(ve.h, 1)
                            ve.inc()

                        @block.vector
                        def _(e):
                            e = AD(e)
                            e.wait_ge(so.h, so.v)
                            e.reciprocal(l_r[:], o_sb[:, 128:129])
                            e.tensor_scalar(o_st[:], o_sb[:, 0:128], l_r[:], None, ALU.mult)
                            e.engine_nop().then_inc(ve.h, 1)
                        v_ost = ve.inc()

                        @block.sync
                        def _(e, iq=iq, h=h, v_ost=v_ost):
                            e.wait_ge(ve.h, v_ost)
                            e.dma_start(out=o1_ts[iq * 128 : iq * 128 + 128,
                                                  h * 128 : h * 128 + 128],
                                        in_=o_st[:]).then_inc(so.h, 16)
                        so.inc(16)

        @block.gpsimd
        def _(e):
            e.wait_ge(so.h, so.v)

    return nc





_NC_CACHE = {}


def _get_nc(name):
    if name not in _NC_CACHE:
        _NC_CACHE[name] = {
            "D": build_dequant, "A0": build_a0, "A1": build_a1,
            "F0": lambda: build_f(I0), "F1": lambda: build_f(I1),
        }[name]()
    return _NC_CACHE[name]


def kernel(L0_attn_x, L0_attn_Wq, L0_attn_Wk, L0_attn_Wv, L0_attn_Wo,
           L0_attn_head_scale, L0_ffn_lut, L0_ffn_gate_idx, L0_ffn_up_idx,
           L0_ffn_down_idx, L1_attn_wq_down, L1_attn_wq_up, L1_attn_wq_rope,
           L1_attn_wkv_down, L1_attn_wk_up, L1_attn_wv_up, L1_attn_wk_rope,
           L1_attn_wo, L1_ffn_wq_gate, L1_ffn_s_gate, L1_ffn_wq_up,
           L1_ffn_s_up, L1_ffn_wq_down, L1_ffn_s_down):
    C = np.ascontiguousarray
    x = L0_attn_x
    hsf = C(L0_attn_head_scale.reshape(16))
    rows_e = [own_rows(0), own_rows(1)]
    thr_e = [np.array([128.0 * ((2 * i + e) % 4) for i in range(16)], np.float32)
             for e in range(2)]

    in_maps = []
    for c in range(8):
        in_maps.append({
            "g0": C(L0_ffn_gate_idx[c * 688 : (c + 1) * 688]),
            "u0": C(L0_ffn_up_idx[c * 688 : (c + 1) * 688]),
            "d0": C(L0_ffn_down_idx[c * 256 : (c + 1) * 256]),
            "lut": C(L0_ffn_lut),
            "g1": C(L1_ffn_wq_gate[c * 704 : (c + 1) * 704]),
            "u1": C(L1_ffn_wq_up[c * 704 : (c + 1) * 704]),
            "d1": C(L1_ffn_wq_down[c * 256 : (c + 1) * 256]),
            "sg": C(L1_ffn_s_gate[c * 704 : (c + 1) * 704]),
            "su": C(L1_ffn_s_up[c * 704 : (c + 1) * 704]),
            "sd": C(L1_ffn_s_down[c * 256 : (c + 1) * 256]),
        })
    res = run_bass_kernel_spmd(_get_nc("D"), in_maps, core_ids=list(range(8)))
    cat = lambda k: np.concatenate([res.results[c][k] for c in range(8)], axis=1)
    wgT0, wuT0, wdT0 = cat("g0t"), cat("u0t"), cat("d0t")
    wgT1, wuT1, wdT1 = cat("g1t"), cat("u1t"), cat("d1t")

    in_maps = []
    for c in range(8):
        b, e = c // 2, c % 2
        in_maps.append({
            "xb": C(x[b]), "x_own": C(x[b][rows_e[e]]),
            "pos_own": rows_e[e].astype(np.int32), "thr": thr_e[e],
            "wq": C(L0_attn_Wq), "wk": C(L0_attn_Wk), "wv": C(L0_attn_Wv),
            "hs": hsf,
        })
    res = run_bass_kernel_spmd(_get_nc("A0"), in_maps, core_ids=list(range(8)))
    o0 = [res.results[c]["o_ts"] for c in range(8)]

    in_maps = []
    for c in range(8):
        b, e = c // 2, c % 2
        in_maps.append({
            "x_own": C(x[b][rows_e[e]]), "o_in": o0[c], "wo": C(L0_attn_Wo),
            "wgT": wgT0, "wuT": wuT0, "wdT": wdT0,
        })
    res = run_bass_kernel_spmd(_get_nc("F0"), in_maps, core_ids=list(range(8)))
    x1f = np.empty((4, S, DM), np.float32)
    for c in range(8):
        b, e = c // 2, c % 2
        x1f[b][rows_e[e]] = res.results[c]["x1"]

    in_maps = []
    for c in range(8):
        b, e = c // 2, c % 2
        in_maps.append({
            "xb": C(x1f[b]), "x_own": C(x1f[b][rows_e[e]]),
            "pos_own": rows_e[e].astype(np.int32), "thr": thr_e[e],
            "wqd": C(L1_attn_wq_down), "wqu": C(L1_attn_wq_up),
            "wqr": C(L1_attn_wq_rope), "wkvd": C(L1_attn_wkv_down),
            "wku": C(L1_attn_wk_up), "wvu": C(L1_attn_wv_up),
            "wkr": C(L1_attn_wk_rope),
        })
    res = run_bass_kernel_spmd(_get_nc("A1"), in_maps, core_ids=list(range(8)))
    o1 = [res.results[c]["o1_ts"] for c in range(8)]

    in_maps = []
    for c in range(8):
        b, e = c // 2, c % 2
        in_maps.append({
            "x_own": C(x1f[b][rows_e[e]]), "o_in": o1[c], "wo": C(L1_attn_wo),
            "wgT": wgT1, "wuT": wuT1, "wdT": wdT1,
        })
    res = run_bass_kernel_spmd(_get_nc("F1"), in_maps, core_ids=list(range(8)))
    out = np.empty((4, S, DM), np.float32)
    for c in range(8):
        b, e = c // 2, c % 2
        out[b][rows_e[e]] = res.results[c]["x1"]
    return out

